# version 10
# baseline (speedup 1.0000x reference)
import numpy as np
import ml_dtypes
import concourse.bass as bass
import concourse.mybir as mybir
from concourse.bass_utils import run_bass_kernel_spmd
from contextlib import ExitStack

F32 = mybir.dt.float32
BF16 = mybir.dt.bfloat16
AF = mybir.ActivationFunctionType
ALU = mybir.AluOpType
AX = mybir.AxisListType

NCORES = 8
P = 128
C = 1024
CCH = 8
H = 16
D = 64
OQKV = 3072
ICH = 11
NE = 10
EPS_A = 1e-6
EPS_LN = 1e-5
GU_GROUPS = [(0, 6), (6, 12), (12, 18), (18, 22)]
WDN_HALVES = [(0, 6), (6, 11)]


class TB:
    COMPUTE = ("pe", "act", "dve", "pool", "sp")

    def __init__(self, nc, ctx):
        self.nc = nc
        self.eng = {"pe": nc.tensor, "act": nc.scalar, "dve": nc.vector,
                    "pool": nc.gpsimd, "sp": nc.sync}
        self.sem = {}
        self.cnt = {}
        for name in self.COMPUTE:
            self.sem[name] = ctx.enter_context(nc.semaphore(f"S_{name}"))
            self.cnt[name] = 0
        self.dcnt = {}
        self.last_w = {}
        self.readers = {}
        self.max_waited = {name: {} for name in self.COMPUTE}
        self.pos = {name: 0 for name in self.COMPUTE}
        self.mat_pos = {name: [] for name in self.COMPUTE}
        self.mat_ent = {name: [] for name in self.COMPUTE}
        self.dma_ents = {}

    def dma_sem(self, name, ctx):
        assert name not in self.sem
        self.sem[name] = ctx.enter_context(self.nc.semaphore(name))
        self.dcnt[name] = 0
        self.dma_ents[name] = []

    def _mat(self, ent):
        if ent["val"] is not None:
            return ent["sem"], ent["val"]
        en = ent["eng"]
        mp = self.mat_pos[en]
        p = ent["pos"]
        if mp and p < mp[-1]:
            import bisect
            i = bisect.bisect_left(mp, p)
            sub = self.mat_ent[en][i]
            return sub["sem"], sub["val"]
        self.cnt[en] += 1
        ent["val"] = self.cnt[en]
        ent["sem"] = en
        ent["inst"].then_inc(self.sem[en], 1)
        mp.append(p)
        self.mat_ent[en].append(ent)
        return en, ent["val"]

    def _deps(self, eng_name, reads, writes):
        deps = {}

        def add(ent):
            s, v = self._mat(ent)
            if deps.get(s, 0) < v:
                deps[s] = v

        for k in reads:
            ent = self.last_w.get(k)
            if ent is not None:
                add(ent)
        for k in writes:
            ent = self.last_w.get(k)
            if ent is not None:
                add(ent)
            for r in self.readers.get(k, ()):
                add(r)
        mw = self.max_waited[eng_name]
        return [(s, v) for s, v in deps.items() if mw.get(s, 0) < v]

    def _register(self, ent, reads, writes):
        for k in writes:
            self.last_w[k] = ent
            self.readers[k] = []
        for k in reads:
            self.readers.setdefault(k, []).append(ent)

    def _emit(self, eng_name, fn, reads, writes, dma_sem=None):
        deps = self._deps(eng_name, reads, writes)
        engobj = self.eng[eng_name]
        mw = self.max_waited[eng_name]
        for s, v in deps[:-1]:
            engobj.wait_ge(self.sem[s], v)
            mw[s] = max(mw.get(s, 0), v)
        inst = fn(engobj)
        if deps:
            s, v = deps[-1]
            inst._wait_ge(self.sem[s], v)
            mw[s] = max(mw.get(s, 0), v)
        if dma_sem is None:
            self.pos[eng_name] += 1
            ent = {"eng": eng_name, "inst": inst, "sem": None, "val": None,
                   "pos": self.pos[eng_name]}
        else:
            self.dcnt[dma_sem] += 16
            inst.then_inc(self.sem[dma_sem], 16)
            ent = {"eng": eng_name, "inst": inst, "sem": dma_sem,
                   "val": self.dcnt[dma_sem]}
            self.dma_ents[dma_sem].append(ent)
        self._register(ent, reads, writes)
        return ent

    def emit(self, eng_name, fn, reads=(), writes=()):
        return self._emit(eng_name, fn, list(reads), list(writes))

    def dma(self, q, semname, fn, reads=(), writes=()):
        return self._emit(q, fn, list(reads), list(writes), dma_sem=semname)

    def alias_key(self, new, old):
        if old in self.last_w:
            self.last_w[new] = self.last_w[old]
        self.readers[new] = list(self.readers.get(old, ()))

    def seal_dma_group(self, semname):
        total = self.dcnt[semname]
        for ent in self.dma_ents[semname]:
            ent["val"] = total


def _span_keys(prefix, off, nbytes, blk=2048):
    return [f"{prefix}{b}" for b in range(off // blk, (off + nbytes - 1) // blk + 1)]


def build_nc(tch=16, dbg=False):
    union = (tch == 16)
    TOK = tch * P
    TW = TOK + 2
    TH = TOK // 2
    nhalf = tch // 2
    QP = [(0, "q"), (1, "q"), (2, "k"), (3, "k"), (4, "v"), (5, "v")]
    R6B = 22528
    O_LN1 = [0, 2048]
    O_QC = [4096, 6144]
    O_KC3 = 8192
    O_KN = 10240
    O_S1 = [12288, 14336]
    O_S2 = [16384, 17408]
    O_VE = 18432
    O_CU = 0
    O_AOT = [4 * TOK, 4 * TOK + 2048]
    O_LN2 = [4 * TOK + 4096, 4 * TOK + 6144]
    O_RT = 4 * TOK + 8192
    assert O_RT + 64 * tch + 2048 <= R6B + 2048

    nc = bass.Bass("TRN2", target_bir_lowering=False)

    xin = nc.dram_tensor("xin", [TOK, C], F32, kind="ExternalInput")
    xhalo = nc.dram_tensor("xhalo", [2, C], F32, kind="ExternalInput")
    hmask = nc.dram_tensor("hmask", [2, 1], F32, kind="ExternalInput")
    wqkvT = nc.dram_tensor("wqkvT", [C, OQKV], F32, kind="ExternalInput")
    wprojT = nc.dram_tensor("wprojT", [C, C], F32, kind="ExternalInput")
    gate9T = nc.dram_tensor("gate9T", [C, 9], F32, kind="ExternalInput")
    bprojr = nc.dram_tensor("bprojr", [1, C], F32, kind="ExternalInput")
    wguT = nc.dram_tensor("wguT", [NE, C, 2816], F32, kind="ExternalInput")
    wdnT = nc.dram_tensor("wdnT", [NE, 1408, C], F32, kind="ExternalInput")
    dwcw = nc.dram_tensor("dwcw", [P, CCH, 3], F32, kind="ExternalInput")
    dwcb = nc.dram_tensor("dwcb", [P, CCH], F32, kind="ExternalInput")
    identw = nc.dram_tensor("identw", [P, P], BF16, kind="ExternalInput")
    y = nc.dram_tensor("y", [TOK, C], F32, kind="ExternalOutput")
    dbgx = nc.dram_tensor("dbgx", [TOK, C], F32, kind="ExternalOutput") if dbg else None
    dbgkv = nc.dram_tensor("dbgkv", [D, H * 65], F32, kind="ExternalOutput") if dbg else None
    dbgkn = nc.dram_tensor("dbgkn", [P, C], BF16, kind="ExternalOutput") if dbg else None
    dbgve = nc.dram_tensor("dbgve", [P, H * 65], BF16, kind="ExternalOutput") if dbg else None
    dbgqc = nc.dram_tensor("dbgqc", [P, C], BF16, kind="ExternalOutput") if dbg else None
    wguH = nc.dram_tensor("wguH", [NE, C, 2816], BF16)
    wdnH = nc.dram_tensor("wdnH", [NE, 1408, C], BF16)
    cc_in = nc.dram_tensor("cc_in", [D, H * 65], F32)
    cc_out = nc.dram_tensor("cc_out", [D, H * 65], F32)

    with ExitStack() as ctx:
        e = ctx.enter_context
        tb = TB(nc, ctx)
        for s in ("dx0", "dx1", "dx2", "dx3", "dxr", "dwq", "dsm", "dsw",
                  "dgu0", "dgu1", "dgu2", "dgu3", "ddn0", "ddn1", "dcc", "dy", "ddbg", "dstA0", "dstA1", "dstg0", "dstg1", "dg9", "dbp", "dpo0", "dpo1", "dbl0", "dbl1", "dbl2", "dbl3", "dbl4"):
            tb.dma_sem(s, ctx)

        rx = e(nc.sbuf_tensor("rx", [P, tch * C], F32))
        if union:
            wqkv_bf = rx[:, 0:12 * C].bitcast(BF16).rearrange(
                "p (j o) -> p j o", o=OQKV)
            wq_keys = lambda j: _span_keys("Rx", j * OQKV * 2, OQKV * 2, 4096)
        else:
            wqkv_t = e(nc.sbuf_tensor("wqkv_t", [P, CCH, OQKV], BF16))
            wqkv_bf = wqkv_t[:, :, :]
            wq_keys = lambda j: [f"wq{j}"]

        xv = lambda t: rx[:, t * C:(t + 1) * C]
        ring_slot = lambda t: (12 + t % 4) if union else t
        xk = lambda t: f"Rx{t}"

        r3 = e(nc.sbuf_tensor("r3", [P, CCH, TW], BF16))
        r4 = e(nc.sbuf_tensor("r4", [P, CCH, 2816], BF16))
        r5 = e(nc.sbuf_tensor("r5", [P, ICH, C], BF16))
        r6 = e(nc.sbuf_tensor("r6", [P, R6B // 2], BF16))

        def r6v(off, dtype, shape):
            n = int(np.prod(shape))
            nbytes = n * mybir.dt.size(dtype)
            ap = r6[:, off // 2: (off + nbytes) // 2]
            if dtype != BF16:
                ap = ap.bitcast(dtype)
            if len(shape) > 1:
                ap = ap.rearrange("p (a b) -> p a b", b=shape[-1])
            return ap

        r6k = lambda off, nbytes: _span_keys("R6", off, nbytes)

        def r4k(j, c0, c1):
            return [f"R4_{j}_{oc}" for oc in range(c0 // P, (c1 - 1) // P + 1)]

        ident = e(nc.sbuf_tensor("ident", [P, P], BF16))
        bnst = e(nc.sbuf_tensor("bnst", [P, 2, 12], F32))
        mv = e(nc.sbuf_tensor("mv", [P, 2], F32))
        mstat = e(nc.sbuf_tensor("mstat", [P, tch + 1, 2], F32))
        sk2 = e(nc.sbuf_tensor("sk2", [P, H], F32))
        kv_sb = e(nc.sbuf_tensor("kv_sb", [P, H * 65], F32))
        kvg_bf = e(nc.sbuf_tensor("kvg_bf", [P, H, 65], BF16))
        g_all = e(nc.sbuf_tensor("g_all", [P, 2, H], F32))
        w10 = e(nc.sbuf_tensor("w10", [P, tch, 10], F32))
        bproj_bf = e(nc.sbuf_tensor("bproj_bf", [1, C], BF16))
        ones_c = e(nc.sbuf_tensor("ones_c", [1, P], BF16))
        gate9_bf = e(nc.sbuf_tensor("gate9_bf", [P, CCH, 9], BF16))
        gate9f = e(nc.sbuf_tensor("gate9f", [P, CCH, 9], F32))
        dwcw_sb = e(nc.sbuf_tensor("dwcw_sb", [P, CCH, 3], F32))
        dwcb_sb = e(nc.sbuf_tensor("dwcb_sb", [P, CCH], F32))
        hm_sb = e(nc.sbuf_tensor("hm_sb", [2, 1], F32))
        cst_eps = e(nc.sbuf_tensor("cst_eps", [P, 1], F32))
        xh_sb = e(nc.sbuf_tensor("xh_sb", [P, C], F32))
        ln1h_tok = e(nc.sbuf_tensor("ln1h_tok", [2, C], BF16))
        ln1h_fm = e(nc.sbuf_tensor("ln1h_fm", [P, CCH, 2], BF16))
        qchT = e(nc.sbuf_tensor("qchT", [P, CCH, 2], BF16))
        gh = e(nc.sbuf_tensor("gh", [2, H], F32))
        hstat = e(nc.sbuf_tensor("hstat", [2, 16], F32))
        qch_tok = ln1h_tok
        aoh_tok = ln1h_tok
        lg_all = r6v(O_RT, F32, (tch, 9))
        ex_all = r6v(O_RT + tch * 36, F32, (tch, 8))
        sc_all = r6v(O_RT + tch * 68, F32, (tch, 8))
        mx8 = r6v(O_RT + tch * 100, F32, (tch, 8))
        rtmp = r6v(O_RT + tch * 132, F32, (tch, 2))
        RTN = tch * 140
        k_lg = r6k(O_RT, tch * 36)
        k_ex = r6k(O_RT + tch * 36, tch * 32)
        k_sc = r6k(O_RT + tch * 68, tch * 32)
        k_mx = r6k(O_RT + tch * 100, tch * 32)
        k_rt = r6k(O_RT + tch * 132, tch * 8)

        ps0 = e(nc.psum_tensor("ps0", [P, 512], F32))
        ps1 = e(nc.psum_tensor("ps1", [P, 512], F32))
        ps2 = e(nc.psum_tensor("ps2", [P, 512], F32))
        ps3 = e(nc.psum_tensor("ps3", [P, 512], F32))
        ps45 = e(nc.psum_tensor("ps45", [P, 1024], F32))
        ps6 = e(nc.psum_tensor("ps6", [P, 512], F32))
        ps7 = e(nc.psum_tensor("ps7", [P, 512], F32))
        kvbank = [(ps0, "pb0", 0, 7), (ps1, "pb1", 7, 14), (ps6, "pb6", 14, 16)]

        block = e(nc.Block())

        @block.vector
        def _build(_v):
            V, A, T, G, S = "dve", "act", "pe", "pool", "sp"

            tb.dma(A, "dsm", lambda s: s.dma_start(out=ident[:, :], in_=identw[:, :]),
                   writes=["ident"])
            tb.dma(A, "dsm", lambda s: s.dma_start(out=xh_sb[0:2, :], in_=xhalo[:, :]),
                   writes=["xh"])
            tb.dma(A, "dsm", lambda s: s.dma_start(out=hm_sb[:, :], in_=hmask[:, :]),
                   writes=["hm"])
            tb.dma(A, "dsm", lambda s: s.dma_start(out=dwcw_sb[:, :, :], in_=dwcw[:, :, :]),
                   writes=["dwcw"])
            tb.dma(A, "dsm", lambda s: s.dma_start(out=dwcb_sb[:, :], in_=dwcb[:, :]),
                   writes=["dwcb"])
            def wqkey(j, c0, c1):
                if union:
                    return _span_keys("Rx", j * OQKV * 2 + c0 * 2, (c1 - c0) * 2, 4096)
                return [f"wq{j}"]

            sta_n = [0]

            def stageA(src_ap, dst_ap, dst_keys):
                sl = sta_n[0] % 2
                sta_n[0] += 1
                col = 2304 + sl * 256
                slot = r4[:, :, col:col + 256].bitcast(F32)
                slk = [f"R4s_{sl}"]
                tb.dma(S, f"dstA{sl}", lambda s, src_ap=src_ap, slot=slot:
                       s.dma_start(out=slot, in_=src_ap), writes=slk)
                tb.emit(V, lambda v, slot=slot, dst_ap=dst_ap: v.tensor_copy(
                    dst_ap, slot), reads=slk, writes=list(dst_keys))

            for oc in range(OQKV // P):
                stageA(wqkvT[:, oc * P:(oc + 1) * P].rearrange("(j p) o -> p j o", p=P),
                       wqkv_bf[:, :, oc * P:(oc + 1) * P],
                       [k for j in range(CCH) for k in wqkey(j, oc * P, (oc + 1) * P)])
            for j in range(CCH):
                stageA(wprojT[j * P:(j + 1) * P, :].rearrange(
                           "p (a o) -> p a o", o=P),
                       r5[:, j, :].rearrange("p (a o) -> p a o", o=P),
                       [f"R5_{j}"])
            tb.dma(A, "dg9", lambda s: s.dma_start(
                out=gate9f[:, :, :], in_=gate9T[:, :].rearrange("(j p) o -> p j o", p=P)),
                writes=["gate9f"])
            tb.emit(V, lambda v: v.tensor_copy(gate9_bf[:, :, :], gate9f[:, :, :]),
                    reads=["gate9f"], writes=["gate9"])
            tb.dma(A, "dbp", lambda s: s.dma_start(
                out=kv_sb[0:1, 0:C], in_=bprojr[:, :]), writes=["kvsb"])
            tb.emit(V, lambda v: v.tensor_copy(bproj_bf[:, :], kv_sb[0:1, 0:C]),
                    reads=["kvsb"], writes=["bproj"])
            tb.seal_dma_group("dsm")
            tb.seal_dma_group("dsw")

            def ring_load(t):
                sl = ring_slot(t)
                tb.dma(A, f"dx{t % 4}",
                       lambda s, t=t, sl=sl: s.dma_start(
                           out=rx[:, sl * C:(sl + 1) * C],
                           in_=xin[t * P:(t + 1) * P, :]),
                       writes=[f"Rx{sl}"])
            for t in range(min(4, tch)):
                ring_load(t)
            tb.emit(G, lambda g: g.memset(ones_c[:, :], 1.0), writes=["ones"])
            tb.emit(V, lambda v: v.memset(cst_eps[:, :], EPS_LN), writes=["csteps"])
            tb.emit(V, lambda v: v.memset(kv_sb[0:D, :], 0.0), writes=["kvsb"])
            tb.emit(V, lambda v: v.memset(
                r6v(O_VE, BF16, (H, 65))[:, :, 64:65], 1.0),
                writes=r6k(O_VE, 2080))
            tb.emit(V, lambda v: v.tensor_scalar(
                dwcw_sb[:, :, 1:2], dwcw_sb[:, :, 1:2], 1.0, None, ALU.add),
                reads=["dwcw"], writes=["dwcw"])

            QDIV = 4 if tch % 4 == 0 else 2
            TQ = TOK // QDIV
            nq = tch // QDIV
            STG = [ICH * TQ * 2, ICH * TQ * 2 + 4096]
            assert STG[1] + 4096 <= R6B
            sgbuf = kv_sb[:, :].bitcast(F32)
            pp_n = [0]

            def prepass_piece(src_ap, dst_ap, dst_key, early):
                sl = pp_n[0] % 2
                pp_n[0] += 1
                if early:
                    f32s = r4[:, :, 2304 + sl * 256:2560 + sl * 256].bitcast(F32)
                    f32k = [f"R4s_{sl}"]
                    dsem_i = f"dstA{sl}"
                else:
                    f32s = r6v(STG[sl], F32, (CCH, P))
                    f32k = r6k(STG[sl], 4096)
                    dsem_i = f"dstg{sl}"
                bfs = xh_sb[:, sl * 512:sl * 512 + 512].bitcast(BF16).rearrange(
                    "p (a o) -> p a o", o=P)
                bfk = [f"xh{sl}"]
                dsem_o = f"dpo{sl}"
                tb.dma(S, dsem_i, lambda s, src_ap=src_ap, f32s=f32s:
                       s.dma_start(out=f32s, in_=src_ap), writes=f32k)
                tb.emit(V, lambda v, f32s=f32s, bfs=bfs: v.tensor_copy(bfs, f32s),
                        reads=f32k, writes=bfk)
                okey = f"{dst_key}_p{pp_n[0]}"
                tb.dma(S, dsem_o, lambda s, bfs=bfs, dst_ap=dst_ap:
                       s.dma_start(out=dst_ap, in_=bfs), reads=bfk,
                       writes=[okey])
                return okey

            ppkeys = {}

            def prepass_expert(ee, early=False):
                ks = []
                for oc in range(2 * ICH):
                    ks.append(prepass_piece(
                        wguT[ee, :, oc * P:(oc + 1) * P].rearrange(
                            "(j p) o -> p j o", p=P),
                        wguH[ee, :, oc * P:(oc + 1) * P].rearrange(
                            "(j p) o -> p j o", p=P),
                        f"wguH{ee}", early))
                ks2 = []
                for m in range(ICH):
                    ks2.append(prepass_piece(
                        wdnT[ee, m * P:(m + 1) * P, :].rearrange(
                            "p (a o) -> p a o", o=P),
                        wdnH[ee, m * P:(m + 1) * P, :].rearrange(
                            "p (a o) -> p a o", o=P),
                        f"wdnH{ee}", early))
                ppkeys[ee] = (ks, ks2)

            WGU_SLABS = [(0, 6), (6, 12), (12, 22)]

            def bf16_load(ee):
                for si, (c0, c1) in enumerate(WGU_SLABS):
                    tb.dma(S, f"dbl{si}", lambda s, ee=ee, c0=c0, c1=c1:
                           s.dma_start(
                               out=r4[:, :, c0 * P:c1 * P],
                               in_=wguH[ee, :, c0 * P:c1 * P].rearrange(
                                   "(j p) o -> p j o", p=P)),
                           reads=ppkeys[ee][0],
                           writes=[f"R4_{j}_{oc}" for j in range(CCH)
                                   for oc in range(c0, c1)]
                           + (["R4s_0", "R4s_1"] if c1 > 18 else []))
                for si, (m0, m1) in enumerate(WDN_HALVES):
                    tb.dma(S, f"dbl{3 + si}", lambda s, ee=ee, m0=m0, m1=m1:
                           s.dma_start(
                               out=r5[:, m0:m1, :],
                               in_=wdnH[ee, m0 * P:m1 * P, :].rearrange(
                                   "(m p) o -> p m o", p=P)),
                           reads=ppkeys[ee][1],
                           writes=[f"R5_{m}" for m in range(m0, m1)])

            def pair_order(ee):
                return list(range(ICH))


            def emit_ln(x_ap, x_keys, pp, stat_idx, out_ap, out_keys, par):
                if pp == P:
                    bn, bnk = bnst[:, par, :], f"bnst{par}"
                    ag, agk = mv[:, :], "mv"
                    ms, msk = mstat[:, stat_idx, :], f"mstat{stat_idx}"
                else:
                    bn, bnk = hstat[0:2, 0:12], "hstat"
                    ag, agk = hstat[0:2, 12:14], "hstat1"
                    ms, msk = hstat[0:2, 14:16], "hstat2"
                tb.emit(V, lambda v: v.bn_stats(bn[:, 0:6], x_ap[:, 0:512]),
                        reads=x_keys, writes=[bnk])
                tb.emit(V, lambda v: v.bn_stats(bn[:, 6:12], x_ap[:, 512:1024]),
                        reads=[bnk] + x_keys, writes=[bnk])
                tb.emit(V, lambda v: v.bn_aggr(ag, bn), reads=[bnk], writes=[agk])
                tb.emit(V, lambda v: v.tensor_copy(ms[:, 0:1], ag[:, 0:1]),
                        reads=[agk], writes=[msk])
                tb.emit(A, lambda a: a.activation(ms[:, 1:2], ag[:, 1:2], AF.Sqrt,
                                                  bias=cst_eps[0:pp, 0:1]),
                        reads=[agk, "csteps"], writes=[msk])
                tb.emit(V, lambda v: v.reciprocal(ms[:, 1:2], ms[:, 1:2]),
                        reads=[msk], writes=[msk])
                tb.emit(V, lambda v: v.tensor_scalar(
                    out_ap, x_ap, ms[:, 0:1], ms[:, 1:2], ALU.subtract, ALU.mult),
                    reads=x_keys + [msk], writes=out_keys)

            def emit_tr8(src_ap, src_keys, dst_ap, dst_keys, pp):
                for j in range(CCH):
                    tb.emit(T, lambda t, j=j: t.matmul(
                        ps45[:, j * pp:(j + 1) * pp],
                        src_ap[:, j * P:(j + 1) * P], ident[0:pp, 0:pp],
                        start=True, stop=True),
                        reads=list(src_keys) + ["ident"],
                        writes=(["pb4", "pb5"] if j in (0, CCH - 1) else []))
                tb.emit(A, lambda a: a.activation(
                    dst_ap, ps45[:, 0:CCH * pp].rearrange("p (j q) -> p j q", q=pp),
                    AF.Copy),
                    reads=["pb4", "pb5"], writes=list(dst_keys))

            def phase_a_chunk(t, halo):
                par = t % 2
                if halo:
                    pp = 2
                    xap, xkeys = xh_sb[0:2, :], ["xh"]
                    ln1, ln1k = ln1h_tok[0:2, :], ["ln1h"]
                    pieces = QP[0:2]
                else:
                    pp = P
                    sl = ring_slot(t)
                    xap, xkeys = rx[:, sl * C:(sl + 1) * C], [f"Rx{sl}"]
                    ln1 = r6v(O_LN1[par], BF16, (C,))
                    ln1k = r6k(O_LN1[par], 2048)
                    pieces = QP
                emit_ln(xap, xkeys, pp, tch if halo else t, ln1, ln1k, par)
                if halo:
                    emit_tr8(ln1, ln1k, ln1h_fm[:, :, :], ["ln1hfm"], 2)
                else:
                    col = 1 + t * P
                    emit_tr8(ln1, ln1k, r3[:, :, col:col + P],
                             [f"R3_{j}" for j in range(CCH)], P)
                if halo:
                    qc, qck = qch_tok, ["ln1h"]
                else:
                    qc = r6v(O_QC[par], BF16, (C,))
                    qck = r6k(O_QC[par], 2048)
                kc3 = r6v(O_KC3, BF16, (C,))
                kc3k = r6k(O_KC3, 2048)
                kn = r6v(O_KN, BF16, (C,))
                knk = r6k(O_KN, 2048)
                ve = r6v(O_VE, BF16, (H, 65))
                vek = r6k(O_VE, 2080)

                for pc, kind in pieces:
                    if halo:
                        pbank, pbk = ps7, "pb7"
                    else:
                        pbank, pbk = (ps2, "pb2") if pc % 2 == 0 else (ps3, "pb3")
                    out_ps = pbank[0:pp, :]
                    for j in range(CCH):
                        if halo:
                            lhsT, lk = ln1h_fm[:, j, 0:2], ["ln1hfm"]
                        else:
                            lhsT, lk = r3[:, j, 1 + t * P:1 + (t + 1) * P], [f"R3_{j}"]
                        tb.emit(T, lambda tt, j=j, pc=pc, lhsT=lhsT, out_ps=out_ps:
                                tt.matmul(out_ps, lhsT,
                                          wqkv_bf[:, j, pc * 512:(pc + 1) * 512],
                                          start=(j == 0), stop=(j == CCH - 1)),
                                reads=lk + wqkey(j, pc * 512, (pc + 1) * 512),
                                writes=([pbk] if j in (0, CCH - 1) else []))
                    spar = pc % 2
                    s1 = r6v(O_S1[spar], F32, (512,))[0:pp, :]
                    s1k = r6k(O_S1[spar], 2048)
                    s2 = r6v(O_S2[spar], BF16, (512,))[0:pp, :]
                    s2k = r6k(O_S2[spar], 1024)
                    off = (pc % 2) * 512
                    if kind == "q":
                        dst, dstk = qc[0:pp, off:off + 512], qck
                    elif kind == "k":
                        dst, dstk = kc3[0:pp, off:off + 512], kc3k
                    if kind in ("q", "k"):
                        tb.emit(V, lambda v, s1=s1, out_ps=out_ps: v.tensor_scalar(
                            s1, out_ps, 0.0, EPS_A, ALU.max, ALU.add),
                            reads=[pbk], writes=s1k)
                        tb.emit(A, lambda a, s1=s1, s2=s2: a.activation(
                            s2, s1, AF.Square), reads=s1k, writes=s2k)
                        tb.emit(G, lambda g, s1=s1, s2=s2, dst=dst: g.tensor_tensor(
                            dst, s2, s1, op=ALU.mult),
                            reads=s1k + s2k, writes=dstk)
                    else:
                        tb.emit(A, lambda a, out_ps=out_ps, pc=pc: a.activation(
                            ve[:, (pc - 4) * 8:(pc - 4) * 8 + 8, 0:64],
                            out_ps.rearrange("p (h d) -> p h d", d=64), AF.Copy),
                            reads=[pbk], writes=vek)
                if halo:
                    return
                tb.emit(V, lambda v: v.tensor_reduce(
                    sk2[:, :].unsqueeze(2),
                    kc3.rearrange("p (h d) -> p h d", d=64),
                    axis=AX.X, op=ALU.add),
                    reads=kc3k, writes=["sk"])
                tb.emit(V, lambda v: v.reciprocal(sk2[:, :], sk2[:, :]),
                        reads=["sk"], writes=["sk"])
                tb.emit(G, lambda g: g.tensor_tensor(
                    kn.rearrange("p (h d) -> p h d", d=64),
                    kc3.rearrange("p (h d) -> p h d", d=64),
                    sk2[:, :].unsqueeze(2).to_broadcast([P, H, 64]), op=ALU.mult),
                    reads=kc3k + ["sk"], writes=knk)
                for h in range(H):
                    bank, bkey, lo, hi = kvbank[h // 7 if h < 14 else 2]
                    off = (h - lo) * 65
                    tb.emit(T, lambda tt, h=h, bank=bank, off=off: tt.matmul(
                                bank[0:D, off:off + 65],
                                kn[:, h * D:(h + 1) * D], ve[:, h, :],
                                start=True, stop=True),
                            reads=knk + vek, writes=[bkey])
                for bank, bkey, lo, hi in kvbank:
                    tb.emit(V, lambda v, bank=bank, lo=lo, hi=hi: v.tensor_tensor(
                        kv_sb[0:D, lo * 65:hi * 65], kv_sb[0:D, lo * 65:hi * 65],
                        bank[0:D, 0:(hi - lo) * 65], op=ALU.add),
                        reads=[bkey, "kvsb"], writes=["kvsb"])
                if dbg and t == 5:
                    tb.dma(S, "ddbg", lambda s: s.dma_start(out=dbgkn[:, :], in_=kn),
                           reads=knk)
                    tb.dma(S, "ddbg", lambda s: s.dma_start(
                        out=dbgve[:, :], in_=ve.rearrange("p a b -> p (a b)")),
                        reads=vek)
                    tb.dma(S, "ddbg", lambda s: s.dma_start(out=dbgqc[:, :], in_=qc),
                           reads=qck)
                    tb.seal_dma_group("ddbg")
                col = 1 + t * P
                emit_tr8(qc, qck, r4[:, :, col:col + P],
                         [k for j in range(CCH) for k in r4k(j, col, col + P)], P)

            phase_a_chunk(0, halo=True)
            emit_tr8(qch_tok, ["ln1h"], qchT[:, :, :], ["qchT"], 2)
            tb.alias_key("xh0", "xh")
            tb.alias_key("xh1", "xh")
            prepass_expert(0, early=True)
            prepass_expert(1, early=True)
            for t in range(tch):
                phase_a_chunk(t, halo=False)
                if t + 4 < tch:
                    ring_load(t + 4)

            tb.dma(G, "dcc", lambda g: g.dma_start(out=cc_in[:, :], in_=kv_sb[0:D, :]),
                   reads=["kvsb"], writes=["ccin"])
            tb.emit(G, lambda g: g.collective_compute(
                "AllReduce", ALU.add, replica_groups=[list(range(NCORES))],
                ins=[cc_in[:, :].opt()], outs=[cc_out[:, :].opt()]),
                reads=["ccin"], writes=["ccout"])
            tb.dma(G, "dcc", lambda g: g.dma_start(out=kv_sb[0:D, :], in_=cc_out[:, :]),
                   reads=["ccout"], writes=["kvsb"])
            tb.dma(G, "dcc", lambda g: g.dma_start(out=kv_sb[D:P, :], in_=cc_out[:, :]),
                   reads=["ccout"], writes=["kvsb"])
            kvv = kv_sb[:, :].rearrange("p (h e) -> p h e", e=65)
            tb.emit(V, lambda v: v.tensor_scalar(
                kvv[:, :, 64:65], kvv[:, :, 64:65], EPS_A, None, ALU.add),
                reads=["kvsb"], writes=["kvsb"])
            tb.emit(V, lambda v: v.tensor_copy(kvg_bf[:, :, :], kvv),
                    reads=["kvsb"], writes=["kvg"])
            if dbg:
                tb.dma(S, "ddbg", lambda s: s.dma_start(
                    out=dbgkv[:, :], in_=kv_sb[0:D, :]), reads=["kvsb"])

            if union:
                for t in range(12):
                    tb.dma(A, "dxr", lambda s, t=t: s.dma_start(
                        out=xv(t), in_=xin[t * P:(t + 1) * P, :]),
                        writes=[xk(t)])
                tb.seal_dma_group("dxr")

            def emit_attn(t, halo):
                pp = 2 if halo else P
                par = 0 if halo else t % 2
                if halo:
                    ao, aok = aoh_tok[0:2, :], ["ln1h"]
                    gg, ggk = gh[0:2, :], "gh"
                else:
                    ao = r6v(O_AOT[par], BF16, (C,))
                    aok = r6k(O_AOT[par], 2048)
                    gg, ggk = g_all[:, par, :], f"gall{par}"
                absets = [kvbank,
                          [(ps2, "pb2", 0, 7), (ps3, "pb3", 7, 14),
                           (ps7, "pb7", 14, 16)]]
                for bank, bkey, lo, hi in absets[0 if halo else t % 2]:
                    nh = hi - lo
                    for h in range(lo, hi):
                        if halo:
                            lhsT = qchT[(h % 2) * D:(h % 2) * D + D, h // 2, 0:2]
                            lk = ["qchT"]
                        else:
                            cc0 = 1 + t * P
                            lhsT = r4[(h % 2) * D:(h % 2) * D + D, h // 2,
                                      cc0:cc0 + P]
                            lk = r4k(h // 2, cc0, cc0 + P)
                        tb.emit(T, lambda tt, bank=bank, h=h, lo=lo, hi=hi,
                                lhsT=lhsT: tt.matmul(
                                    bank[0:pp, (h - lo) * 65:(h - lo) * 65 + 65],
                                    lhsT, kvg_bf[(h % 2) * D:(h % 2) * D + D, h, :],
                                    start=True, stop=True),
                                reads=lk + ["kvg"], writes=[bkey])
                    bview = bank[0:pp, 0:nh * 65].rearrange("p (h e) -> p h e", e=65)
                    tb.emit(V, lambda v, bview=bview, lo=lo, hi=hi: v.reciprocal(
                        gg[0:pp, lo:hi], bview[:, :, 64:65].squeeze(2)),
                        reads=[bkey], writes=[ggk])
                    tb.emit(V, lambda v, bview=bview, lo=lo, hi=hi, nh=nh:
                            v.tensor_tensor(
                                ao[0:pp, lo * D:hi * D].rearrange(
                                    "p (h d) -> p h d", d=D),
                                bview[:, :, 0:64],
                                gg[0:pp, lo:hi].unsqueeze(2).to_broadcast(
                                    [pp, nh, D]),
                                op=ALU.mult),
                            reads=[bkey, ggk], writes=aok)
                if halo:
                    tb.emit(V, lambda v: v.tensor_scalar(
                        ao[0:2, :], ao[0:2, :], hm_sb[0:2, 0:1], None, ALU.mult),
                        reads=aok + ["hm"], writes=aok)
                    for j in range(CCH):
                        tb.emit(T, lambda tt, j=j: tt.matmul(
                            ps45[:, j * 2:(j + 1) * 2], ao[0:2, j * P:(j + 1) * P],
                            ident[0:2, 0:2], start=True, stop=True),
                            reads=aok + ["ident"],
                            writes=(["pb4", "pb5"] if j in (0, CCH - 1) else []))
                    pv = ps45[:, 0:2 * CCH].rearrange("p (j q) -> p j q", q=2)
                    tb.emit(A, lambda a: a.activation(
                        r3[:, :, 0:1], pv[:, :, 0:1], AF.Copy),
                        reads=["pb4", "pb5"],
                        writes=[f"R3_{j}" for j in range(CCH)])
                    tb.emit(A, lambda a: a.activation(
                        r3[:, :, TW - 1:TW], pv[:, :, 1:2], AF.Copy),
                        reads=["pb4", "pb5"],
                        writes=[f"R3_{j}" for j in range(CCH)])
                else:
                    col = 1 + t * P
                    emit_tr8(ao, aok, r3[:, :, col:col + P],
                             [f"R3_{j}" for j in range(CCH)], P)

            for t in range(tch):
                emit_attn(t, halo=False)
            emit_attn(0, halo=True)

            for j in range(CCH):
                u = r6v(O_CU, F32, (TOK,))
                uk = r6k(O_CU, 4 * TOK)
                tb.emit(V, lambda v, j=j, u=u: v.tensor_scalar(
                    u, r3[:, j, 2:2 + TOK], dwcw_sb[:, j, 2:3], dwcb_sb[:, j:j + 1],
                    ALU.mult, ALU.add),
                    reads=[f"R3_{j}", "dwcw", "dwcb"], writes=uk)
                tb.emit(V, lambda v, j=j, u=u: v.scalar_tensor_tensor(
                    u, r3[:, j, 0:TOK], dwcw_sb[:, j, 0:1], u, ALU.mult, ALU.add),
                    reads=[f"R3_{j}", "dwcw"] + uk, writes=uk)
                tb.emit(V, lambda v, j=j, u=u: v.scalar_tensor_tensor(
                    r4[:, j, 0:TOK], r3[:, j, 1:1 + TOK], dwcw_sb[:, j, 1:2], u,
                    ALU.mult, ALU.add),
                    reads=[f"R3_{j}", "dwcw"] + uk, writes=r4k(j, 0, TOK))

            def emit_proj(t):
                for piece in range(2):
                    bank, bkey = (ps2, "pb2") if piece == 0 else (ps3, "pb3")
                    for j in range(CCH):
                        tb.emit(T, lambda tt, j=j, piece=piece, t=t, bank=bank:
                                tt.matmul(bank[:, :],
                                          r4[:, j, t * P:(t + 1) * P],
                                          r5[:, j, piece * 512:(piece + 1) * 512],
                                          start=(j == 0), stop=False),
                                reads=r4k(j, t * P, (t + 1) * P) + [f"R5_{j}"],
                                writes=([bkey] if j == 0 else []))
                    tb.emit(T, lambda tt, piece=piece, bank=bank: tt.matmul(
                        bank[:, :], ones_c[0:1, :],
                        bproj_bf[0:1, piece * 512:(piece + 1) * 512],
                        start=False, stop=True),
                        reads=["ones", "bproj"], writes=[bkey])
                    tb.emit(V, lambda v, t=t, piece=piece, bank=bank: v.tensor_tensor(
                        xv(t)[:, piece * 512:(piece + 1) * 512],
                        xv(t)[:, piece * 512:(piece + 1) * 512],
                        bank[:, :], op=ALU.add),
                        reads=[xk(t), bkey], writes=[xk(t)])

            def emit_ln2chain(t):
                par = t % 2
                ln2 = r6v(O_LN2[par], BF16, (C,))
                ln2k = r6k(O_LN2[par], 2048)
                emit_ln(xv(t), [xk(t)], P, t, ln2, ln2k, par)

            def emit_trgate(t):
                par = t % 2
                ln2 = r6v(O_LN2[par], BF16, (C,))
                ln2k = r6k(O_LN2[par], 2048)
                col = t * P
                emit_tr8(ln2, ln2k, r3[:, :, col:col + P],
                         [f"R3_{j}" for j in range(CCH)], P)
                for j in range(CCH):
                    tb.emit(T, lambda tt, j=j, t=t: tt.matmul(
                        ps7[:, 0:9], r3[:, j, t * P:(t + 1) * P], gate9_bf[:, j, :],
                        start=(j == 0), stop=(j == CCH - 1)),
                        reads=[f"R3_{j}", "gate9"],
                        writes=(["pb7"] if j in (0, CCH - 1) else []))
                tb.emit(A, lambda a, t=t: a.activation(
                    lg_all[:, t, :], ps7[:, 0:9], AF.Copy),
                    reads=["pb7"], writes=k_lg)

            for t in range(tch):
                emit_proj(t)
                emit_ln2chain(t)
                if t > 0:
                    emit_trgate(t - 1)
            emit_trgate(tch - 1)

            if dbg:
                for t in range(tch):
                    tb.dma(S, "ddbg", lambda s, t=t: s.dma_start(
                        out=dbgx[t * P:(t + 1) * P, :], in_=xv(t)),
                        reads=[xk(t)])
                tb.seal_dma_group("ddbg")

            tb.emit(V, lambda v: v.tensor_reduce(
                rtmp[:, :, 0:1], lg_all[:, :, 0:8], axis=AX.X, op=ALU.max,
                negate=True),
                reads=k_lg, writes=k_rt)
            for t in range(tch):
                tb.emit(A, lambda a, t=t: a.activation(
                    ex_all[:, t, :], lg_all[:, t, 0:8], AF.Exp,
                    bias=rtmp[:, t, 0:1]),
                    reads=k_lg + k_rt, writes=k_ex)
            tb.emit(V, lambda v: v.tensor_reduce(
                rtmp[:, :, 1:2], ex_all[:, :, :], axis=AX.X, op=ALU.add),
                reads=k_ex, writes=k_rt)
            tb.emit(V, lambda v: v.reciprocal(rtmp[:, :, 1:2], rtmp[:, :, 1:2]),
                    reads=k_rt, writes=k_rt)
            tb.emit(V, lambda v: v.tensor_tensor(
                sc_all[:, :, :], ex_all[:, :, :],
                rtmp[:, :, 1:2].to_broadcast([P, tch, 8]), op=ALU.mult),
                reads=k_ex + k_rt, writes=k_sc)
            for t in range(tch):
                tb.emit(V, lambda v, t=t: v.max(out=mx8[:, t, :],
                                                in_=sc_all[:, t, :]),
                        reads=k_sc, writes=k_mx)
            tb.emit(V, lambda v: v.tensor_tensor(
                ex_all[:, :, :], sc_all[:, :, :],
                mx8[:, :, 1:2].to_broadcast([P, tch, 8]), op=ALU.is_ge),
                reads=k_sc + k_mx, writes=k_ex)
            tb.emit(V, lambda v: v.tensor_tensor(
                w10[:, :, 0:8], sc_all[:, :, :], ex_all[:, :, :], op=ALU.mult),
                reads=k_sc + k_ex, writes=["w10"])
            tb.emit(A, lambda a: a.activation(
                w10[:, :, 8:9], lg_all[:, :, 8:9], AF.Sigmoid),
                reads=k_lg, writes=["w10"])
            tb.emit(V, lambda v: v.tensor_copy(w10[:, :, 9:10], w10[:, :, 8:9]),
                    reads=["w10"], writes=["w10"])

            tb.alias_key("kvsb0", "kvsb")
            tb.alias_key("kvsb1", "kvsb")
            bf16_load(0)
            for ee in range(NE):
                po = pair_order(ee)
                for qr in range(QDIV):
                    base = qr * TQ
                    for mi, m in enumerate(po):
                        gb, gk = (ps0, "pb0") if mi % 2 == 0 else (ps2, "pb2")
                        ub, uk_ = (ps1, "pb1") if mi % 2 == 0 else (ps3, "pb3")
                        hrow = r6v(m * TQ * 2, BF16, (TQ,))
                        hk = r6k(m * TQ * 2, TQ * 2)
                        for bank, bky, oc in ((gb, gk, 2 * m), (ub, uk_, 2 * m + 1)):
                            for j in range(CCH):
                                tb.emit(T, lambda tt, j=j, oc=oc, bank=bank,
                                        base=base: tt.matmul(
                                            bank[:, 0:TQ],
                                            r4[:, j, oc * P:(oc + 1) * P],
                                            r3[:, j, base:base + TQ],
                                            start=(j == 0), stop=(j == CCH - 1)),
                                        reads=[f"R4_{j}_{oc}", f"R3_{j}"],
                                        writes=([bky] if j in (0, CCH - 1) else []))
                        sg = sgbuf[:, (mi % 2) * 512:(mi % 2) * 512 + TQ]
                        sgk = [f"kvsb{mi % 2}"]
                        tb.emit(A, lambda a, sg=sg, gb=gb: a.activation(
                            sg[:, 0:TQ], gb[:, 0:TQ], AF.Silu),
                            reads=[gk], writes=sgk)
                        tb.emit(V, lambda v, sg=sg, ub=ub, hrow=hrow:
                                v.tensor_tensor(hrow[:, 0:TQ], sg[:, 0:TQ],
                                                ub[:, 0:TQ], op=ALU.mult),
                                reads=sgk + [uk_], writes=hk)
                    for ti in range(nq):
                        t = qr * nq + ti
                        for piece in range(2):
                            db = ps45[:, 0:512] if piece == 0 else ps45[:, 512:1024]
                            dbk = "pb4" if piece == 0 else "pb5"
                            for m in range(ICH):
                                hrow = r6v(m * TQ * 2, BF16, (TQ,))
                                hk = r6k(m * TQ * 2, TQ * 2)
                                tb.emit(T, lambda tt, m=m, ti=ti, piece=piece,
                                        db=db, hrow=hrow: tt.matmul(
                                            db, hrow[:, ti * P:(ti + 1) * P],
                                            r5[:, m, piece * 512:(piece + 1) * 512],
                                            start=(m == 0), stop=(m == ICH - 1)),
                                        reads=hk + [f"R5_{m}"],
                                        writes=([dbk] if m in (0, ICH - 1) else []))
                            tb.emit(V, lambda v, t=t, piece=piece, db=db, ee=ee:
                                    v.scalar_tensor_tensor(
                                        xv(t)[:, piece * 512:(piece + 1) * 512],
                                        db, w10[:, t, ee:ee + 1],
                                        xv(t)[:, piece * 512:(piece + 1) * 512],
                                        ALU.mult, ALU.add),
                                    reads=[dbk, "w10", xk(t)], writes=[xk(t)])
                if ee + 1 < NE:
                    bf16_load(ee + 1)
                if ee + 2 < NE:
                    prepass_expert(ee + 2)

            for t in range(tch):
                tb.dma(S, "dy", lambda s, t=t: s.dma_start(
                    out=y[t * P:(t + 1) * P, :], in_=xv(t)),
                    reads=[xk(t)])
            tb.eng["sp"].wait_ge(tb.sem["dy"], tb.dcnt["dy"])

    return nc


def _prep_inputs(L0_attn_x, L0_attn_w_qkv, L0_attn_w_proj, L0_attn_b_proj,
                 L0_attn_w_dwc, L0_attn_b_dwc, L0_ffn_gate_w,
                 L0_ffn_experts_gate_up, L0_ffn_experts_down,
                 L0_ffn_shared_gate_w, L0_ffn_shared_up_w, L0_ffn_shared_down_w,
                 L0_ffn_shared_expert_gate_w):
    f32 = np.float32
    x = np.ascontiguousarray(np.asarray(L0_attn_x, dtype=f32).reshape(-1, C))
    ntok = x.shape[0]
    per = ntok // NCORES
    wqkvT_ = np.ascontiguousarray(np.asarray(L0_attn_w_qkv, f32).T)
    wprojT_ = np.ascontiguousarray(np.asarray(L0_attn_w_proj, f32).T)
    gate9 = np.concatenate([np.asarray(L0_ffn_gate_w, f32),
                            np.asarray(L0_ffn_shared_expert_gate_w, f32)], axis=0)
    gate9T_ = np.ascontiguousarray(gate9.T)
    bprojr_ = np.ascontiguousarray(np.asarray(L0_attn_b_proj, f32).reshape(1, C))
    order = []
    for m in range(ICH):
        order += [m, ICH + m]
    egu = np.asarray(L0_ffn_experts_gate_up, f32)
    sg = np.asarray(L0_ffn_shared_gate_w, f32)
    su = np.asarray(L0_ffn_shared_up_w, f32)
    sd = np.asarray(L0_ffn_shared_down_w, f32)
    edn = np.asarray(L0_ffn_experts_down, f32)
    wguT_ = np.empty((NE, C, 2816), f32)
    wdnT_ = np.empty((NE, 1408, C), f32)
    for ee in range(8):
        rows = np.concatenate([egu[ee][o * P:(o + 1) * P] for o in order], axis=0)
        wguT_[ee] = rows.T
        wdnT_[ee] = edn[ee].T
    for pi in range(2):
        r0, r1 = pi * 1408, (pi + 1) * 1408
        stacked = np.concatenate([sg[r0:r1], su[r0:r1]], axis=0)
        rows = np.concatenate([stacked[o * P:(o + 1) * P] for o in order], axis=0)
        wguT_[8 + pi] = rows.T
        wdnT_[8 + pi] = np.ascontiguousarray(sd[:, r0:r1].T)
    wdc = np.asarray(L0_attn_w_dwc, f32)
    dwcw_ = np.ascontiguousarray(wdc.reshape(CCH, P, 3).transpose(1, 0, 2))
    dwcb_ = np.ascontiguousarray(np.asarray(L0_attn_b_dwc, f32).reshape(CCH, P).T)
    identw_ = np.eye(P, dtype=ml_dtypes.bfloat16)

    in_maps = []
    for i in range(NCORES):
        s = i * per
        hl = x[s - 1] if i > 0 else np.zeros(C, f32)
        hr = x[s + per] if i < NCORES - 1 else np.zeros(C, f32)
        in_maps.append({
            "xin": np.ascontiguousarray(x[s:s + per]),
            "xhalo": np.ascontiguousarray(np.stack([hl, hr])),
            "hmask": np.array([[1.0 if i > 0 else 0.0],
                               [1.0 if i < NCORES - 1 else 0.0]], f32),
            "wqkvT": wqkvT_, "wprojT": wprojT_, "gate9T": gate9T_,
            "bprojr": bprojr_, "wguT": wguT_, "wdnT": wdnT_,
            "dwcw": dwcw_, "dwcb": dwcb_, "identw": identw_,
        })
    return in_maps, x.shape


def kernel(**inputs):
    in_maps, (ntok, _) = _prep_inputs(**inputs)
    tch = ntok // NCORES // P
    nc = build_nc(tch=tch)
    res = run_bass_kernel_spmd(nc, in_maps, core_ids=list(range(NCORES)))
    out = np.concatenate([r["y"] for r in res.results], axis=0)
    return out.reshape(np.asarray(inputs["L0_attn_x"]).shape)


# version 11
# speedup vs baseline: 1.0177x; 1.0177x over previous
import numpy as np
import ml_dtypes
import concourse.bass as bass
import concourse.mybir as mybir
from concourse.bass_utils import run_bass_kernel_spmd
from contextlib import ExitStack

F32 = mybir.dt.float32
BF16 = mybir.dt.bfloat16
AF = mybir.ActivationFunctionType
ALU = mybir.AluOpType
AX = mybir.AxisListType

NCORES = 8
P = 128
C = 1024
CCH = 8
H = 16
D = 64
OQKV = 3072
ICH = 11
NE = 10
EPS_A = 1e-6
EPS_LN = 1e-5
GU_GROUPS = [(0, 6), (6, 12), (12, 18), (18, 22)]
WDN_HALVES = [(0, 6), (6, 11)]


class TB:
    COMPUTE = ("pe", "act", "dve", "pool", "sp")

    def __init__(self, nc, ctx):
        self.nc = nc
        self.eng = {"pe": nc.tensor, "act": nc.scalar, "dve": nc.vector,
                    "pool": nc.gpsimd, "sp": nc.sync}
        self.sem = {}
        self.cnt = {}
        for name in self.COMPUTE:
            self.sem[name] = ctx.enter_context(nc.semaphore(f"S_{name}"))
            self.cnt[name] = 0
        self.dcnt = {}
        self.last_w = {}
        self.readers = {}
        self.max_waited = {name: {} for name in self.COMPUTE}
        self.pos = {name: 0 for name in self.COMPUTE}
        self.mat_pos = {name: [] for name in self.COMPUTE}
        self.mat_ent = {name: [] for name in self.COMPUTE}
        self.dma_ents = {}

    def dma_sem(self, name, ctx):
        assert name not in self.sem
        self.sem[name] = ctx.enter_context(self.nc.semaphore(name))
        self.dcnt[name] = 0
        self.dma_ents[name] = []

    def _mat(self, ent):
        if ent["val"] is not None:
            return ent["sem"], ent["val"]
        en = ent["eng"]
        mp = self.mat_pos[en]
        p = ent["pos"]
        if mp and p < mp[-1]:
            import bisect
            i = bisect.bisect_left(mp, p)
            sub = self.mat_ent[en][i]
            return sub["sem"], sub["val"]
        self.cnt[en] += 1
        ent["val"] = self.cnt[en]
        ent["sem"] = en
        ent["inst"].then_inc(self.sem[en], 1)
        mp.append(p)
        self.mat_ent[en].append(ent)
        return en, ent["val"]

    def _deps(self, eng_name, reads, writes):
        deps = {}

        def add(ent):
            s, v = self._mat(ent)
            if deps.get(s, 0) < v:
                deps[s] = v

        for k in reads:
            ent = self.last_w.get(k)
            if ent is not None:
                add(ent)
        for k in writes:
            ent = self.last_w.get(k)
            if ent is not None:
                add(ent)
            for r in self.readers.get(k, ()):
                add(r)
        mw = self.max_waited[eng_name]
        return [(s, v) for s, v in deps.items() if mw.get(s, 0) < v]

    def _register(self, ent, reads, writes):
        for k in writes:
            self.last_w[k] = ent
            self.readers[k] = []
        for k in reads:
            self.readers.setdefault(k, []).append(ent)

    def _emit(self, eng_name, fn, reads, writes, dma_sem=None):
        deps = self._deps(eng_name, reads, writes)
        engobj = self.eng[eng_name]
        mw = self.max_waited[eng_name]
        for s, v in deps[:-1]:
            engobj.wait_ge(self.sem[s], v)
            mw[s] = max(mw.get(s, 0), v)
        inst = fn(engobj)
        if deps:
            s, v = deps[-1]
            inst._wait_ge(self.sem[s], v)
            mw[s] = max(mw.get(s, 0), v)
        if dma_sem is None:
            self.pos[eng_name] += 1
            ent = {"eng": eng_name, "inst": inst, "sem": None, "val": None,
                   "pos": self.pos[eng_name]}
        else:
            self.dcnt[dma_sem] += 16
            inst.then_inc(self.sem[dma_sem], 16)
            ent = {"eng": eng_name, "inst": inst, "sem": dma_sem,
                   "val": self.dcnt[dma_sem]}
            self.dma_ents[dma_sem].append(ent)
        self._register(ent, reads, writes)
        return ent

    def emit(self, eng_name, fn, reads=(), writes=()):
        return self._emit(eng_name, fn, list(reads), list(writes))

    def dma(self, q, semname, fn, reads=(), writes=()):
        return self._emit(q, fn, list(reads), list(writes), dma_sem=semname)

    def alias_key(self, new, old):
        if old in self.last_w:
            self.last_w[new] = self.last_w[old]
        self.readers[new] = list(self.readers.get(old, ()))

    def seal_dma_group(self, semname):
        total = self.dcnt[semname]
        for ent in self.dma_ents[semname]:
            ent["val"] = total


def _span_keys(prefix, off, nbytes, blk=2048):
    return [f"{prefix}{b}" for b in range(off // blk, (off + nbytes - 1) // blk + 1)]


def build_nc(tch=16, dbg=False):
    union = (tch == 16)
    TOK = tch * P
    TW = TOK + 2
    TH = TOK // 2
    nhalf = tch // 2
    QP = [(0, "q"), (1, "q"), (2, "k"), (3, "k"), (4, "v"), (5, "v")]
    R6B = 22528
    O_LN1 = [0, 2048]
    O_QC = [4096, 6144]
    O_KC3 = 8192
    O_KN = 10240
    O_S1 = [12288, 14336]
    O_S2 = [16384, 17408]
    O_VE = 18432
    O_CU = 0
    O_AOT = [4 * TOK, 4 * TOK + 2048]
    O_LN2 = [4 * TOK + 4096, 4 * TOK + 6144]
    O_RT = 4 * TOK + 8192
    assert O_RT + 64 * tch + 2048 <= R6B + 2048

    nc = bass.Bass("TRN2", target_bir_lowering=False)

    xin = nc.dram_tensor("xin", [TOK, C], F32, kind="ExternalInput")
    xhalo = nc.dram_tensor("xhalo", [2, C], F32, kind="ExternalInput")
    hmask = nc.dram_tensor("hmask", [2, 1], F32, kind="ExternalInput")
    wqkvT = nc.dram_tensor("wqkvT", [C, OQKV], F32, kind="ExternalInput")
    wprojT = nc.dram_tensor("wprojT", [C, C], F32, kind="ExternalInput")
    gate9T = nc.dram_tensor("gate9T", [C, 9], F32, kind="ExternalInput")
    bprojr = nc.dram_tensor("bprojr", [1, C], F32, kind="ExternalInput")
    wguT = nc.dram_tensor("wguT", [NE, C, 2816], F32, kind="ExternalInput")
    wdnT = nc.dram_tensor("wdnT", [NE, 1408, C], F32, kind="ExternalInput")
    dwcw = nc.dram_tensor("dwcw", [P, CCH, 3], F32, kind="ExternalInput")
    dwcb = nc.dram_tensor("dwcb", [P, CCH], F32, kind="ExternalInput")
    identw = nc.dram_tensor("identw", [P, P], BF16, kind="ExternalInput")
    y = nc.dram_tensor("y", [TOK, C], F32, kind="ExternalOutput")
    dbgx = nc.dram_tensor("dbgx", [TOK, C], F32, kind="ExternalOutput") if dbg else None
    dbgkv = nc.dram_tensor("dbgkv", [D, H * 65], F32, kind="ExternalOutput") if dbg else None
    dbgkn = nc.dram_tensor("dbgkn", [P, C], BF16, kind="ExternalOutput") if dbg else None
    dbgve = nc.dram_tensor("dbgve", [P, H * 65], BF16, kind="ExternalOutput") if dbg else None
    dbgqc = nc.dram_tensor("dbgqc", [P, C], BF16, kind="ExternalOutput") if dbg else None
    wguH = nc.dram_tensor("wguH", [NE, C, 2816], BF16)
    wdnH = nc.dram_tensor("wdnH", [NE, 1408, C], BF16)
    cc_in = nc.dram_tensor("cc_in", [D, H * 65], F32)
    cc_out = nc.dram_tensor("cc_out", [D, H * 65], F32)

    with ExitStack() as ctx:
        e = ctx.enter_context
        tb = TB(nc, ctx)
        for s in ("dx0", "dx1", "dx2", "dx3", "dxr", "dwq", "dsm", "dsw",
                  "dgu0", "dgu1", "dgu2", "dgu3", "ddn0", "ddn1", "dcc", "dy", "ddbg", "dstA0", "dstA1", "dstg0", "dstg1", "dg9", "dbp", "dpo0", "dpo1", "dbl0", "dbl1", "dbl2", "dbl3", "dbl4"):
            tb.dma_sem(s, ctx)

        rx = e(nc.sbuf_tensor("rx", [P, tch * C], F32))
        if union:
            wqkv_bf = rx[:, 0:12 * C].bitcast(BF16).rearrange(
                "p (j o) -> p j o", o=OQKV)
            wq_keys = lambda j: _span_keys("Rx", j * OQKV * 2, OQKV * 2, 4096)
        else:
            wqkv_t = e(nc.sbuf_tensor("wqkv_t", [P, CCH, OQKV], BF16))
            wqkv_bf = wqkv_t[:, :, :]
            wq_keys = lambda j: [f"wq{j}"]

        xv = lambda t: rx[:, t * C:(t + 1) * C]
        ring_slot = lambda t: (12 + t % 4) if union else t
        xk = lambda t: f"Rx{t}"

        r3 = e(nc.sbuf_tensor("r3", [P, CCH, TW], BF16))
        r4 = e(nc.sbuf_tensor("r4", [P, CCH, 2816], BF16))
        r5 = e(nc.sbuf_tensor("r5", [P, ICH, C], BF16))
        r6 = e(nc.sbuf_tensor("r6", [P, R6B // 2], BF16))

        def r6v(off, dtype, shape):
            n = int(np.prod(shape))
            nbytes = n * mybir.dt.size(dtype)
            ap = r6[:, off // 2: (off + nbytes) // 2]
            if dtype != BF16:
                ap = ap.bitcast(dtype)
            if len(shape) > 1:
                ap = ap.rearrange("p (a b) -> p a b", b=shape[-1])
            return ap

        r6k = lambda off, nbytes: _span_keys("R6", off, nbytes)

        def r4k(j, c0, c1):
            return [f"R4_{j}_{oc}" for oc in range(c0 // P, (c1 - 1) // P + 1)]

        ident = e(nc.sbuf_tensor("ident", [P, P], BF16))
        bnst = e(nc.sbuf_tensor("bnst", [P, 2, 12], F32))
        mv = e(nc.sbuf_tensor("mv", [P, 2], F32))
        mstat = e(nc.sbuf_tensor("mstat", [P, tch + 1, 2], F32))
        sk2 = e(nc.sbuf_tensor("sk2", [P, H], F32))
        kv_sb = e(nc.sbuf_tensor("kv_sb", [P, H * 65], F32))
        kvg_bf = e(nc.sbuf_tensor("kvg_bf", [P, H, 65], BF16))
        g_all = e(nc.sbuf_tensor("g_all", [P, 2, H], F32))
        w10 = e(nc.sbuf_tensor("w10", [P, tch, 10], F32))
        bproj_bf = e(nc.sbuf_tensor("bproj_bf", [1, C], BF16))
        ones_c = e(nc.sbuf_tensor("ones_c", [1, P], BF16))
        gate9_bf = e(nc.sbuf_tensor("gate9_bf", [P, CCH, 9], BF16))
        gate9f = e(nc.sbuf_tensor("gate9f", [P, CCH, 9], F32))
        dwcw_sb = e(nc.sbuf_tensor("dwcw_sb", [P, CCH, 3], F32))
        dwcb_sb = e(nc.sbuf_tensor("dwcb_sb", [P, CCH], F32))
        hm_sb = e(nc.sbuf_tensor("hm_sb", [2, 1], F32))
        cst_eps = e(nc.sbuf_tensor("cst_eps", [P, 1], F32))
        xh_sb = e(nc.sbuf_tensor("xh_sb", [P, C], F32))
        ln1h_tok = e(nc.sbuf_tensor("ln1h_tok", [2, C], BF16))
        ln1h_fm = e(nc.sbuf_tensor("ln1h_fm", [P, CCH, 2], BF16))
        qchT = e(nc.sbuf_tensor("qchT", [P, CCH, 2], BF16))
        gh = e(nc.sbuf_tensor("gh", [2, H], F32))
        hstat = e(nc.sbuf_tensor("hstat", [2, 16], F32))
        qch_tok = ln1h_tok
        aoh_tok = ln1h_tok
        lg_all = r6v(O_RT, F32, (tch, 9))
        ex_all = r6v(O_RT + tch * 36, F32, (tch, 8))
        sc_all = r6v(O_RT + tch * 68, F32, (tch, 8))
        mx8 = r6v(O_RT + tch * 100, F32, (tch, 8))
        rtmp = r6v(O_RT + tch * 132, F32, (tch, 2))
        RTN = tch * 140
        k_lg = r6k(O_RT, tch * 36)
        k_ex = r6k(O_RT + tch * 36, tch * 32)
        k_sc = r6k(O_RT + tch * 68, tch * 32)
        k_mx = r6k(O_RT + tch * 100, tch * 32)
        k_rt = r6k(O_RT + tch * 132, tch * 8)

        ps0 = e(nc.psum_tensor("ps0", [P, 512], F32))
        ps1 = e(nc.psum_tensor("ps1", [P, 512], F32))
        ps2 = e(nc.psum_tensor("ps2", [P, 512], F32))
        ps3 = e(nc.psum_tensor("ps3", [P, 512], F32))
        ps45 = e(nc.psum_tensor("ps45", [P, 1024], F32))
        ps6 = e(nc.psum_tensor("ps6", [P, 512], F32))
        ps7 = e(nc.psum_tensor("ps7", [P, 512], F32))
        kvbank = [(ps0, "pb0", 0, 7), (ps1, "pb1", 7, 14), (ps6, "pb6", 14, 16)]

        block = e(nc.Block())

        @block.vector
        def _build(_v):
            V, A, T, G, S = "dve", "act", "pe", "pool", "sp"

            tb.dma(A, "dsm", lambda s: s.dma_start(out=ident[:, :], in_=identw[:, :]),
                   writes=["ident"])
            tb.dma(A, "dsm", lambda s: s.dma_start(out=xh_sb[0:2, :], in_=xhalo[:, :]),
                   writes=["xh"])
            tb.dma(A, "dsm", lambda s: s.dma_start(out=hm_sb[:, :], in_=hmask[:, :]),
                   writes=["hm"])
            tb.dma(A, "dsm", lambda s: s.dma_start(out=dwcw_sb[:, :, :], in_=dwcw[:, :, :]),
                   writes=["dwcw"])
            tb.dma(A, "dsm", lambda s: s.dma_start(out=dwcb_sb[:, :], in_=dwcb[:, :]),
                   writes=["dwcb"])
            def wqkey(j, c0, c1):
                if union:
                    return _span_keys("Rx", j * OQKV * 2 + c0 * 2, (c1 - c0) * 2, 4096)
                return [f"wq{j}"]

            sta_n = [0]

            def stageA(src_ap, dst_ap, dst_keys):
                sl = sta_n[0] % 2
                sta_n[0] += 1
                col = 2304 + sl * 256
                slot = r4[:, :, col:col + 256].bitcast(F32)
                slk = [f"R4s_{sl}"]
                tb.dma(S, f"dstA{sl}", lambda s, src_ap=src_ap, slot=slot:
                       s.dma_start(out=slot, in_=src_ap), writes=slk)
                tb.emit(G, lambda g, slot=slot, dst_ap=dst_ap: g.tensor_copy(
                    dst_ap, slot), reads=slk, writes=list(dst_keys))

            for oc in range(OQKV // P):
                stageA(wqkvT[:, oc * P:(oc + 1) * P].rearrange("(j p) o -> p j o", p=P),
                       wqkv_bf[:, :, oc * P:(oc + 1) * P],
                       [k for j in range(CCH) for k in wqkey(j, oc * P, (oc + 1) * P)])
            for j in range(CCH):
                stageA(wprojT[j * P:(j + 1) * P, :].rearrange(
                           "p (a o) -> p a o", o=P),
                       r5[:, j, :].rearrange("p (a o) -> p a o", o=P),
                       [f"R5_{j}"])
            tb.dma(A, "dg9", lambda s: s.dma_start(
                out=gate9f[:, :, :], in_=gate9T[:, :].rearrange("(j p) o -> p j o", p=P)),
                writes=["gate9f"])
            tb.emit(V, lambda v: v.tensor_copy(gate9_bf[:, :, :], gate9f[:, :, :]),
                    reads=["gate9f"], writes=["gate9"])
            tb.dma(A, "dbp", lambda s: s.dma_start(
                out=kv_sb[0:1, 0:C], in_=bprojr[:, :]), writes=["kvsb"])
            tb.emit(V, lambda v: v.tensor_copy(bproj_bf[:, :], kv_sb[0:1, 0:C]),
                    reads=["kvsb"], writes=["bproj"])
            tb.seal_dma_group("dsm")
            tb.seal_dma_group("dsw")

            def ring_load(t):
                sl = ring_slot(t)
                tb.dma(A, f"dx{t % 4}",
                       lambda s, t=t, sl=sl: s.dma_start(
                           out=rx[:, sl * C:(sl + 1) * C],
                           in_=xin[t * P:(t + 1) * P, :]),
                       writes=[f"Rx{sl}"])
            for t in range(min(4, tch)):
                ring_load(t)
            tb.emit(G, lambda g: g.memset(ones_c[:, :], 1.0), writes=["ones"])
            tb.emit(V, lambda v: v.memset(cst_eps[:, :], EPS_LN), writes=["csteps"])
            tb.emit(V, lambda v: v.memset(kv_sb[0:D, :], 0.0), writes=["kvsb"])
            tb.emit(V, lambda v: v.memset(
                r6v(O_VE, BF16, (H, 65))[:, :, 64:65], 1.0),
                writes=r6k(O_VE, 2080))
            tb.emit(V, lambda v: v.tensor_scalar(
                dwcw_sb[:, :, 1:2], dwcw_sb[:, :, 1:2], 1.0, None, ALU.add),
                reads=["dwcw"], writes=["dwcw"])

            QDIV = 4 if tch % 4 == 0 else 2
            TQ = TOK // QDIV
            nq = tch // QDIV
            STG = [ICH * TQ * 2, ICH * TQ * 2 + 4096]
            assert STG[1] + 4096 <= R6B
            sgbuf = kv_sb[:, :].bitcast(F32)
            pp_n = [0]

            def prepass_piece(src_ap, dst_ap, dst_key, early):
                sl = pp_n[0] % 2
                pp_n[0] += 1
                if early:
                    f32s = r4[:, :, 2304 + sl * 256:2560 + sl * 256].bitcast(F32)
                    f32k = [f"R4s_{sl}"]
                    dsem_i = f"dstA{sl}"
                else:
                    f32s = r6v(STG[sl], F32, (CCH, P))
                    f32k = r6k(STG[sl], 4096)
                    dsem_i = f"dstg{sl}"
                bfs = xh_sb[:, sl * 512:sl * 512 + 512].bitcast(BF16).rearrange(
                    "p (a o) -> p a o", o=P)
                bfk = [f"xh{sl}"]
                dsem_o = f"dpo{sl}"
                tb.dma(S, dsem_i, lambda s, src_ap=src_ap, f32s=f32s:
                       s.dma_start(out=f32s, in_=src_ap), writes=f32k)
                tb.emit(V, lambda v, f32s=f32s, bfs=bfs: v.tensor_copy(bfs, f32s),
                        reads=f32k, writes=bfk)
                okey = f"{dst_key}_p{pp_n[0]}"
                tb.dma(S, dsem_o, lambda s, bfs=bfs, dst_ap=dst_ap:
                       s.dma_start(out=dst_ap, in_=bfs), reads=bfk,
                       writes=[okey])
                return okey

            ppkeys = {}

            def prepass_args(ee):
                args = []
                for oc in range(2 * ICH):
                    args.append(("gu", ee,
                                 wguT[ee, :, oc * P:(oc + 1) * P].rearrange(
                                     "(j p) o -> p j o", p=P),
                                 wguH[ee, :, oc * P:(oc + 1) * P].rearrange(
                                     "(j p) o -> p j o", p=P)))
                for m in range(ICH):
                    args.append(("dn", ee,
                                 wdnT[ee, m * P:(m + 1) * P, :].rearrange(
                                     "p (a o) -> p a o", o=P),
                                 wdnH[ee, m * P:(m + 1) * P, :].rearrange(
                                     "p (a o) -> p a o", o=P)))
                return args

            def emit_prepass(args, early):
                for kind, ee, src_ap, dst_ap in args:
                    ppkeys.setdefault(ee, ([], []))
                    k = prepass_piece(src_ap, dst_ap,
                                      f"wguH{ee}" if kind == "gu" else f"wdnH{ee}",
                                      early)
                    ppkeys[ee][0 if kind == "gu" else 1].append(k)

            def prepass_expert(ee, early=False):
                emit_prepass(prepass_args(ee), early)

            WGU_SLABS = [(0, 6), (6, 12), (12, 22)]

            def bf16_load(ee):
                for si, (c0, c1) in enumerate(WGU_SLABS):
                    tb.dma(A, f"dbl{si}", lambda s, ee=ee, c0=c0, c1=c1:
                           s.dma_start(
                               out=r4[:, :, c0 * P:c1 * P],
                               in_=wguH[ee, :, c0 * P:c1 * P].rearrange(
                                   "(j p) o -> p j o", p=P)),
                           reads=ppkeys[ee][0],
                           writes=[f"R4_{j}_{oc}" for j in range(CCH)
                                   for oc in range(c0, c1)]
                           + (["R4s_0", "R4s_1"] if c1 > 18 else []))
                for si, (m0, m1) in enumerate(WDN_HALVES):
                    tb.dma(A, f"dbl{3 + si}", lambda s, ee=ee, m0=m0, m1=m1:
                           s.dma_start(
                               out=r5[:, m0:m1, :],
                               in_=wdnH[ee, m0 * P:m1 * P, :].rearrange(
                                   "(m p) o -> p m o", p=P)),
                           reads=ppkeys[ee][1],
                           writes=[f"R5_{m}" for m in range(m0, m1)])

            def pair_order(ee):
                return list(range(ICH))


            def emit_ln(x_ap, x_keys, pp, stat_idx, out_ap, out_keys, par):
                if pp == P:
                    bn, bnk = bnst[:, par, :], f"bnst{par}"
                    ag, agk = mv[:, :], "mv"
                    ms, msk = mstat[:, stat_idx, :], f"mstat{stat_idx}"
                else:
                    bn, bnk = hstat[0:2, 0:12], "hstat"
                    ag, agk = hstat[0:2, 12:14], "hstat1"
                    ms, msk = hstat[0:2, 14:16], "hstat2"
                tb.emit(V, lambda v: v.bn_stats(bn[:, 0:6], x_ap[:, 0:512]),
                        reads=x_keys, writes=[bnk])
                tb.emit(V, lambda v: v.bn_stats(bn[:, 6:12], x_ap[:, 512:1024]),
                        reads=[bnk] + x_keys, writes=[bnk])
                tb.emit(V, lambda v: v.bn_aggr(ag, bn), reads=[bnk], writes=[agk])
                tb.emit(V, lambda v: v.tensor_copy(ms[:, 0:1], ag[:, 0:1]),
                        reads=[agk], writes=[msk])
                tb.emit(A, lambda a: a.activation(ms[:, 1:2], ag[:, 1:2], AF.Sqrt,
                                                  bias=cst_eps[0:pp, 0:1]),
                        reads=[agk, "csteps"], writes=[msk])
                tb.emit(V, lambda v: v.reciprocal(ms[:, 1:2], ms[:, 1:2]),
                        reads=[msk], writes=[msk])
                tb.emit(V, lambda v: v.tensor_scalar(
                    out_ap, x_ap, ms[:, 0:1], ms[:, 1:2], ALU.subtract, ALU.mult),
                    reads=x_keys + [msk], writes=out_keys)

            def emit_tr8(src_ap, src_keys, dst_ap, dst_keys, pp):
                for j in range(CCH):
                    tb.emit(T, lambda t, j=j: t.matmul(
                        ps45[:, j * pp:(j + 1) * pp],
                        src_ap[:, j * P:(j + 1) * P], ident[0:pp, 0:pp],
                        start=True, stop=True),
                        reads=list(src_keys) + ["ident"],
                        writes=(["pb4", "pb5"] if j in (0, CCH - 1) else []))
                tb.emit(A, lambda a: a.activation(
                    dst_ap, ps45[:, 0:CCH * pp].rearrange("p (j q) -> p j q", q=pp),
                    AF.Copy),
                    reads=["pb4", "pb5"], writes=list(dst_keys))

            def phase_a_chunk(t, halo):
                par = t % 2
                if halo:
                    pp = 2
                    xap, xkeys = xh_sb[0:2, :], ["xh"]
                    ln1, ln1k = ln1h_tok[0:2, :], ["ln1h"]
                    pieces = QP[0:2]
                else:
                    pp = P
                    sl = ring_slot(t)
                    xap, xkeys = rx[:, sl * C:(sl + 1) * C], [f"Rx{sl}"]
                    ln1 = r6v(O_LN1[par], BF16, (C,))
                    ln1k = r6k(O_LN1[par], 2048)
                    pieces = QP
                emit_ln(xap, xkeys, pp, tch if halo else t, ln1, ln1k, par)
                if halo:
                    emit_tr8(ln1, ln1k, ln1h_fm[:, :, :], ["ln1hfm"], 2)
                else:
                    col = 1 + t * P
                    emit_tr8(ln1, ln1k, r3[:, :, col:col + P],
                             [f"R3_{j}" for j in range(CCH)], P)
                if halo:
                    qc, qck = qch_tok, ["ln1h"]
                else:
                    qc = r6v(O_QC[par], BF16, (C,))
                    qck = r6k(O_QC[par], 2048)
                kc3 = r6v(O_KC3, BF16, (C,))
                kc3k = r6k(O_KC3, 2048)
                kn = r6v(O_KN, BF16, (C,))
                knk = r6k(O_KN, 2048)
                ve = r6v(O_VE, BF16, (H, 65))
                vek = r6k(O_VE, 2080)

                for pc, kind in pieces:
                    if halo:
                        pbank, pbk = ps7, "pb7"
                    else:
                        pbank, pbk = (ps2, "pb2") if pc % 2 == 0 else (ps3, "pb3")
                    out_ps = pbank[0:pp, :]
                    for j in range(CCH):
                        if halo:
                            lhsT, lk = ln1h_fm[:, j, 0:2], ["ln1hfm"]
                        else:
                            lhsT, lk = r3[:, j, 1 + t * P:1 + (t + 1) * P], [f"R3_{j}"]
                        tb.emit(T, lambda tt, j=j, pc=pc, lhsT=lhsT, out_ps=out_ps:
                                tt.matmul(out_ps, lhsT,
                                          wqkv_bf[:, j, pc * 512:(pc + 1) * 512],
                                          start=(j == 0), stop=(j == CCH - 1)),
                                reads=lk + wqkey(j, pc * 512, (pc + 1) * 512),
                                writes=([pbk] if j in (0, CCH - 1) else []))
                    spar = pc % 2
                    s1 = r6v(O_S1[spar], F32, (512,))[0:pp, :]
                    s1k = r6k(O_S1[spar], 2048)
                    s2 = r6v(O_S2[spar], BF16, (512,))[0:pp, :]
                    s2k = r6k(O_S2[spar], 1024)
                    off = (pc % 2) * 512
                    if kind == "q":
                        dst, dstk = qc[0:pp, off:off + 512], qck
                    elif kind == "k":
                        dst, dstk = kc3[0:pp, off:off + 512], kc3k
                    if kind in ("q", "k"):
                        tb.emit(V, lambda v, s1=s1, out_ps=out_ps: v.tensor_scalar(
                            s1, out_ps, 0.0, EPS_A, ALU.max, ALU.add),
                            reads=[pbk], writes=s1k)
                        tb.emit(A, lambda a, s1=s1, s2=s2: a.activation(
                            s2, s1, AF.Square), reads=s1k, writes=s2k)
                        tb.emit(G, lambda g, s1=s1, s2=s2, dst=dst: g.tensor_tensor(
                            dst, s2, s1, op=ALU.mult),
                            reads=s1k + s2k, writes=dstk)
                    else:
                        tb.emit(A, lambda a, out_ps=out_ps, pc=pc: a.activation(
                            ve[:, (pc - 4) * 8:(pc - 4) * 8 + 8, 0:64],
                            out_ps.rearrange("p (h d) -> p h d", d=64), AF.Copy),
                            reads=[pbk], writes=vek)
                if halo:
                    return
                tb.emit(V, lambda v: v.tensor_reduce(
                    sk2[:, :].unsqueeze(2),
                    kc3.rearrange("p (h d) -> p h d", d=64),
                    axis=AX.X, op=ALU.add),
                    reads=kc3k, writes=["sk"])
                tb.emit(V, lambda v: v.reciprocal(sk2[:, :], sk2[:, :]),
                        reads=["sk"], writes=["sk"])
                tb.emit(G, lambda g: g.tensor_tensor(
                    kn.rearrange("p (h d) -> p h d", d=64),
                    kc3.rearrange("p (h d) -> p h d", d=64),
                    sk2[:, :].unsqueeze(2).to_broadcast([P, H, 64]), op=ALU.mult),
                    reads=kc3k + ["sk"], writes=knk)
                for h in range(H):
                    bank, bkey, lo, hi = kvbank[h // 7 if h < 14 else 2]
                    off = (h - lo) * 65
                    tb.emit(T, lambda tt, h=h, bank=bank, off=off: tt.matmul(
                                bank[0:D, off:off + 65],
                                kn[:, h * D:(h + 1) * D], ve[:, h, :],
                                start=True, stop=True),
                            reads=knk + vek, writes=[bkey])
                for bank, bkey, lo, hi in kvbank:
                    tb.emit(V, lambda v, bank=bank, lo=lo, hi=hi: v.tensor_tensor(
                        kv_sb[0:D, lo * 65:hi * 65], kv_sb[0:D, lo * 65:hi * 65],
                        bank[0:D, 0:(hi - lo) * 65], op=ALU.add),
                        reads=[bkey, "kvsb"], writes=["kvsb"])
                if dbg and t == 5:
                    tb.dma(S, "ddbg", lambda s: s.dma_start(out=dbgkn[:, :], in_=kn),
                           reads=knk)
                    tb.dma(S, "ddbg", lambda s: s.dma_start(
                        out=dbgve[:, :], in_=ve.rearrange("p a b -> p (a b)")),
                        reads=vek)
                    tb.dma(S, "ddbg", lambda s: s.dma_start(out=dbgqc[:, :], in_=qc),
                           reads=qck)
                    tb.seal_dma_group("ddbg")
                col = 1 + t * P
                emit_tr8(qc, qck, r4[:, :, col:col + P],
                         [k for j in range(CCH) for k in r4k(j, col, col + P)], P)

            phase_a_chunk(0, halo=True)
            emit_tr8(qch_tok, ["ln1h"], qchT[:, :, :], ["qchT"], 2)
            tb.alias_key("xh0", "xh")
            tb.alias_key("xh1", "xh")
            early_args = prepass_args(0) + prepass_args(1)
            per_t = (len(early_args) + tch - 1) // tch
            for t in range(tch):
                phase_a_chunk(t, halo=False)
                if t + 4 < tch:
                    ring_load(t + 4)
                emit_prepass(early_args[t * per_t:(t + 1) * per_t], early=True)

            tb.dma(A, "dcc", lambda g: g.dma_start(out=cc_in[:, :], in_=kv_sb[0:D, :]),
                   reads=["kvsb"], writes=["ccin"])
            tb.emit(G, lambda g: g.collective_compute(
                "AllReduce", ALU.add, replica_groups=[list(range(NCORES))],
                ins=[cc_in[:, :].opt()], outs=[cc_out[:, :].opt()]),
                reads=["ccin"], writes=["ccout"])
            tb.dma(A, "dcc", lambda g: g.dma_start(out=kv_sb[0:D, :], in_=cc_out[:, :]),
                   reads=["ccout"], writes=["kvsb"])
            tb.dma(A, "dcc", lambda g: g.dma_start(out=kv_sb[D:P, :], in_=cc_out[:, :]),
                   reads=["ccout"], writes=["kvsb"])
            kvv = kv_sb[:, :].rearrange("p (h e) -> p h e", e=65)
            tb.emit(V, lambda v: v.tensor_scalar(
                kvv[:, :, 64:65], kvv[:, :, 64:65], EPS_A, None, ALU.add),
                reads=["kvsb"], writes=["kvsb"])
            tb.emit(V, lambda v: v.tensor_copy(kvg_bf[:, :, :], kvv),
                    reads=["kvsb"], writes=["kvg"])
            if dbg:
                tb.dma(S, "ddbg", lambda s: s.dma_start(
                    out=dbgkv[:, :], in_=kv_sb[0:D, :]), reads=["kvsb"])

            if union:
                for t in range(12):
                    tb.dma(A, "dxr", lambda s, t=t: s.dma_start(
                        out=xv(t), in_=xin[t * P:(t + 1) * P, :]),
                        writes=[xk(t)])
                tb.seal_dma_group("dxr")

            def emit_attn(t, halo):
                pp = 2 if halo else P
                par = 0 if halo else t % 2
                if halo:
                    ao, aok = aoh_tok[0:2, :], ["ln1h"]
                    gg, ggk = gh[0:2, :], "gh"
                else:
                    ao = r6v(O_AOT[par], BF16, (C,))
                    aok = r6k(O_AOT[par], 2048)
                    gg, ggk = g_all[:, par, :], f"gall{par}"
                absets = [kvbank,
                          [(ps2, "pb2", 0, 7), (ps3, "pb3", 7, 14),
                           (ps7, "pb7", 14, 16)]]
                for bank, bkey, lo, hi in absets[0 if halo else t % 2]:
                    nh = hi - lo
                    for h in range(lo, hi):
                        if halo:
                            lhsT = qchT[(h % 2) * D:(h % 2) * D + D, h // 2, 0:2]
                            lk = ["qchT"]
                        else:
                            cc0 = 1 + t * P
                            lhsT = r4[(h % 2) * D:(h % 2) * D + D, h // 2,
                                      cc0:cc0 + P]
                            lk = r4k(h // 2, cc0, cc0 + P)
                        tb.emit(T, lambda tt, bank=bank, h=h, lo=lo, hi=hi,
                                lhsT=lhsT: tt.matmul(
                                    bank[0:pp, (h - lo) * 65:(h - lo) * 65 + 65],
                                    lhsT, kvg_bf[(h % 2) * D:(h % 2) * D + D, h, :],
                                    start=True, stop=True),
                                reads=lk + ["kvg"], writes=[bkey])
                    bview = bank[0:pp, 0:nh * 65].rearrange("p (h e) -> p h e", e=65)
                    tb.emit(V, lambda v, bview=bview, lo=lo, hi=hi: v.reciprocal(
                        gg[0:pp, lo:hi], bview[:, :, 64:65].squeeze(2)),
                        reads=[bkey], writes=[ggk])
                    tb.emit(V, lambda v, bview=bview, lo=lo, hi=hi, nh=nh:
                            v.tensor_tensor(
                                ao[0:pp, lo * D:hi * D].rearrange(
                                    "p (h d) -> p h d", d=D),
                                bview[:, :, 0:64],
                                gg[0:pp, lo:hi].unsqueeze(2).to_broadcast(
                                    [pp, nh, D]),
                                op=ALU.mult),
                            reads=[bkey, ggk], writes=aok)
                if halo:
                    tb.emit(V, lambda v: v.tensor_scalar(
                        ao[0:2, :], ao[0:2, :], hm_sb[0:2, 0:1], None, ALU.mult),
                        reads=aok + ["hm"], writes=aok)
                    for j in range(CCH):
                        tb.emit(T, lambda tt, j=j: tt.matmul(
                            ps45[:, j * 2:(j + 1) * 2], ao[0:2, j * P:(j + 1) * P],
                            ident[0:2, 0:2], start=True, stop=True),
                            reads=aok + ["ident"],
                            writes=(["pb4", "pb5"] if j in (0, CCH - 1) else []))
                    pv = ps45[:, 0:2 * CCH].rearrange("p (j q) -> p j q", q=2)
                    tb.emit(A, lambda a: a.activation(
                        r3[:, :, 0:1], pv[:, :, 0:1], AF.Copy),
                        reads=["pb4", "pb5"],
                        writes=[f"R3_{j}" for j in range(CCH)])
                    tb.emit(A, lambda a: a.activation(
                        r3[:, :, TW - 1:TW], pv[:, :, 1:2], AF.Copy),
                        reads=["pb4", "pb5"],
                        writes=[f"R3_{j}" for j in range(CCH)])
                else:
                    col = 1 + t * P
                    emit_tr8(ao, aok, r3[:, :, col:col + P],
                             [f"R3_{j}" for j in range(CCH)], P)

            for t in range(tch):
                emit_attn(t, halo=False)
            emit_attn(0, halo=True)

            for j in range(CCH):
                u = r6v(O_CU, F32, (TOK,))
                uk = r6k(O_CU, 4 * TOK)
                tb.emit(V, lambda v, j=j, u=u: v.tensor_scalar(
                    u, r3[:, j, 2:2 + TOK], dwcw_sb[:, j, 2:3], dwcb_sb[:, j:j + 1],
                    ALU.mult, ALU.add),
                    reads=[f"R3_{j}", "dwcw", "dwcb"], writes=uk)
                tb.emit(V, lambda v, j=j, u=u: v.scalar_tensor_tensor(
                    u, r3[:, j, 0:TOK], dwcw_sb[:, j, 0:1], u, ALU.mult, ALU.add),
                    reads=[f"R3_{j}", "dwcw"] + uk, writes=uk)
                tb.emit(V, lambda v, j=j, u=u: v.scalar_tensor_tensor(
                    r4[:, j, 0:TOK], r3[:, j, 1:1 + TOK], dwcw_sb[:, j, 1:2], u,
                    ALU.mult, ALU.add),
                    reads=[f"R3_{j}", "dwcw"] + uk, writes=r4k(j, 0, TOK))

            def emit_proj(t):
                for piece in range(2):
                    bank, bkey = (ps2, "pb2") if piece == 0 else (ps3, "pb3")
                    for j in range(CCH):
                        tb.emit(T, lambda tt, j=j, piece=piece, t=t, bank=bank:
                                tt.matmul(bank[:, :],
                                          r4[:, j, t * P:(t + 1) * P],
                                          r5[:, j, piece * 512:(piece + 1) * 512],
                                          start=(j == 0), stop=False),
                                reads=r4k(j, t * P, (t + 1) * P) + [f"R5_{j}"],
                                writes=([bkey] if j == 0 else []))
                    tb.emit(T, lambda tt, piece=piece, bank=bank: tt.matmul(
                        bank[:, :], ones_c[0:1, :],
                        bproj_bf[0:1, piece * 512:(piece + 1) * 512],
                        start=False, stop=True),
                        reads=["ones", "bproj"], writes=[bkey])
                    tb.emit(V, lambda v, t=t, piece=piece, bank=bank: v.tensor_tensor(
                        xv(t)[:, piece * 512:(piece + 1) * 512],
                        xv(t)[:, piece * 512:(piece + 1) * 512],
                        bank[:, :], op=ALU.add),
                        reads=[xk(t), bkey], writes=[xk(t)])

            def emit_ln2chain(t):
                par = t % 2
                ln2 = r6v(O_LN2[par], BF16, (C,))
                ln2k = r6k(O_LN2[par], 2048)
                emit_ln(xv(t), [xk(t)], P, t, ln2, ln2k, par)

            def emit_trgate(t):
                par = t % 2
                ln2 = r6v(O_LN2[par], BF16, (C,))
                ln2k = r6k(O_LN2[par], 2048)
                col = t * P
                emit_tr8(ln2, ln2k, r3[:, :, col:col + P],
                         [f"R3_{j}" for j in range(CCH)], P)
                for j in range(CCH):
                    tb.emit(T, lambda tt, j=j, t=t: tt.matmul(
                        ps7[:, 0:9], r3[:, j, t * P:(t + 1) * P], gate9_bf[:, j, :],
                        start=(j == 0), stop=(j == CCH - 1)),
                        reads=[f"R3_{j}", "gate9"],
                        writes=(["pb7"] if j in (0, CCH - 1) else []))
                tb.emit(A, lambda a, t=t: a.activation(
                    lg_all[:, t, :], ps7[:, 0:9], AF.Copy),
                    reads=["pb7"], writes=k_lg)

            for t in range(tch):
                emit_proj(t)
                emit_ln2chain(t)
                if t > 0:
                    emit_trgate(t - 1)
            emit_trgate(tch - 1)

            if dbg:
                for t in range(tch):
                    tb.dma(S, "ddbg", lambda s, t=t: s.dma_start(
                        out=dbgx[t * P:(t + 1) * P, :], in_=xv(t)),
                        reads=[xk(t)])
                tb.seal_dma_group("ddbg")

            tb.emit(V, lambda v: v.tensor_reduce(
                rtmp[:, :, 0:1], lg_all[:, :, 0:8], axis=AX.X, op=ALU.max,
                negate=True),
                reads=k_lg, writes=k_rt)
            for t in range(tch):
                tb.emit(A, lambda a, t=t: a.activation(
                    ex_all[:, t, :], lg_all[:, t, 0:8], AF.Exp,
                    bias=rtmp[:, t, 0:1]),
                    reads=k_lg + k_rt, writes=k_ex)
            tb.emit(V, lambda v: v.tensor_reduce(
                rtmp[:, :, 1:2], ex_all[:, :, :], axis=AX.X, op=ALU.add),
                reads=k_ex, writes=k_rt)
            tb.emit(V, lambda v: v.reciprocal(rtmp[:, :, 1:2], rtmp[:, :, 1:2]),
                    reads=k_rt, writes=k_rt)
            tb.emit(V, lambda v: v.tensor_tensor(
                sc_all[:, :, :], ex_all[:, :, :],
                rtmp[:, :, 1:2].to_broadcast([P, tch, 8]), op=ALU.mult),
                reads=k_ex + k_rt, writes=k_sc)
            for t in range(tch):
                tb.emit(V, lambda v, t=t: v.max(out=mx8[:, t, :],
                                                in_=sc_all[:, t, :]),
                        reads=k_sc, writes=k_mx)
            tb.emit(V, lambda v: v.tensor_tensor(
                ex_all[:, :, :], sc_all[:, :, :],
                mx8[:, :, 1:2].to_broadcast([P, tch, 8]), op=ALU.is_ge),
                reads=k_sc + k_mx, writes=k_ex)
            tb.emit(V, lambda v: v.tensor_tensor(
                w10[:, :, 0:8], sc_all[:, :, :], ex_all[:, :, :], op=ALU.mult),
                reads=k_sc + k_ex, writes=["w10"])
            tb.emit(A, lambda a: a.activation(
                w10[:, :, 8:9], lg_all[:, :, 8:9], AF.Sigmoid),
                reads=k_lg, writes=["w10"])
            tb.emit(V, lambda v: v.tensor_copy(w10[:, :, 9:10], w10[:, :, 8:9]),
                    reads=["w10"], writes=["w10"])

            tb.alias_key("kvsb0", "kvsb")
            tb.alias_key("kvsb1", "kvsb")
            bf16_load(0)
            for ee in range(NE):
                po = pair_order(ee)
                for qr in range(QDIV):
                    base = qr * TQ
                    for mi, m in enumerate(po):
                        gb, gk = (ps0, "pb0") if mi % 2 == 0 else (ps2, "pb2")
                        ub, uk_ = (ps1, "pb1") if mi % 2 == 0 else (ps3, "pb3")
                        hrow = r6v(m * TQ * 2, BF16, (TQ,))
                        hk = r6k(m * TQ * 2, TQ * 2)
                        for bank, bky, oc in ((gb, gk, 2 * m), (ub, uk_, 2 * m + 1)):
                            for j in range(CCH):
                                tb.emit(T, lambda tt, j=j, oc=oc, bank=bank,
                                        base=base: tt.matmul(
                                            bank[:, 0:TQ],
                                            r4[:, j, oc * P:(oc + 1) * P],
                                            r3[:, j, base:base + TQ],
                                            start=(j == 0), stop=(j == CCH - 1)),
                                        reads=[f"R4_{j}_{oc}", f"R3_{j}"],
                                        writes=([bky] if j in (0, CCH - 1) else []))
                        sg = sgbuf[:, (mi % 2) * 512:(mi % 2) * 512 + TQ]
                        sgk = [f"kvsb{mi % 2}"]
                        tb.emit(A, lambda a, sg=sg, gb=gb: a.activation(
                            sg[:, 0:TQ], gb[:, 0:TQ], AF.Silu),
                            reads=[gk], writes=sgk)
                        tb.emit(V, lambda v, sg=sg, ub=ub, hrow=hrow:
                                v.tensor_tensor(hrow[:, 0:TQ], sg[:, 0:TQ],
                                                ub[:, 0:TQ], op=ALU.mult),
                                reads=sgk + [uk_], writes=hk)
                    for ti in range(nq):
                        t = qr * nq + ti
                        for piece in range(2):
                            db = ps45[:, 0:512] if piece == 0 else ps45[:, 512:1024]
                            dbk = "pb4" if piece == 0 else "pb5"
                            for m in range(ICH):
                                hrow = r6v(m * TQ * 2, BF16, (TQ,))
                                hk = r6k(m * TQ * 2, TQ * 2)
                                tb.emit(T, lambda tt, m=m, ti=ti, piece=piece,
                                        db=db, hrow=hrow: tt.matmul(
                                            db, hrow[:, ti * P:(ti + 1) * P],
                                            r5[:, m, piece * 512:(piece + 1) * 512],
                                            start=(m == 0), stop=(m == ICH - 1)),
                                        reads=hk + [f"R5_{m}"],
                                        writes=([dbk] if m in (0, ICH - 1) else []))
                            tb.emit(V, lambda v, t=t, piece=piece, db=db, ee=ee:
                                    v.scalar_tensor_tensor(
                                        xv(t)[:, piece * 512:(piece + 1) * 512],
                                        db, w10[:, t, ee:ee + 1],
                                        xv(t)[:, piece * 512:(piece + 1) * 512],
                                        ALU.mult, ALU.add),
                                    reads=[dbk, "w10", xk(t)], writes=[xk(t)])
                if ee + 1 < NE:
                    bf16_load(ee + 1)
                if ee + 2 < NE:
                    prepass_expert(ee + 2)

            for t in range(tch):
                tb.dma(S, "dy", lambda s, t=t: s.dma_start(
                    out=y[t * P:(t + 1) * P, :], in_=xv(t)),
                    reads=[xk(t)])
            tb.eng["sp"].wait_ge(tb.sem["dy"], tb.dcnt["dy"])

    return nc


def _prep_inputs(L0_attn_x, L0_attn_w_qkv, L0_attn_w_proj, L0_attn_b_proj,
                 L0_attn_w_dwc, L0_attn_b_dwc, L0_ffn_gate_w,
                 L0_ffn_experts_gate_up, L0_ffn_experts_down,
                 L0_ffn_shared_gate_w, L0_ffn_shared_up_w, L0_ffn_shared_down_w,
                 L0_ffn_shared_expert_gate_w):
    f32 = np.float32
    x = np.ascontiguousarray(np.asarray(L0_attn_x, dtype=f32).reshape(-1, C))
    ntok = x.shape[0]
    per = ntok // NCORES
    wqkvT_ = np.ascontiguousarray(np.asarray(L0_attn_w_qkv, f32).T)
    wprojT_ = np.ascontiguousarray(np.asarray(L0_attn_w_proj, f32).T)
    gate9 = np.concatenate([np.asarray(L0_ffn_gate_w, f32),
                            np.asarray(L0_ffn_shared_expert_gate_w, f32)], axis=0)
    gate9T_ = np.ascontiguousarray(gate9.T)
    bprojr_ = np.ascontiguousarray(np.asarray(L0_attn_b_proj, f32).reshape(1, C))
    order = []
    for m in range(ICH):
        order += [m, ICH + m]
    egu = np.asarray(L0_ffn_experts_gate_up, f32)
    sg = np.asarray(L0_ffn_shared_gate_w, f32)
    su = np.asarray(L0_ffn_shared_up_w, f32)
    sd = np.asarray(L0_ffn_shared_down_w, f32)
    edn = np.asarray(L0_ffn_experts_down, f32)
    wguT_ = np.empty((NE, C, 2816), f32)
    wdnT_ = np.empty((NE, 1408, C), f32)
    for ee in range(8):
        rows = np.concatenate([egu[ee][o * P:(o + 1) * P] for o in order], axis=0)
        wguT_[ee] = rows.T
        wdnT_[ee] = edn[ee].T
    for pi in range(2):
        r0, r1 = pi * 1408, (pi + 1) * 1408
        stacked = np.concatenate([sg[r0:r1], su[r0:r1]], axis=0)
        rows = np.concatenate([stacked[o * P:(o + 1) * P] for o in order], axis=0)
        wguT_[8 + pi] = rows.T
        wdnT_[8 + pi] = np.ascontiguousarray(sd[:, r0:r1].T)
    wdc = np.asarray(L0_attn_w_dwc, f32)
    dwcw_ = np.ascontiguousarray(wdc.reshape(CCH, P, 3).transpose(1, 0, 2))
    dwcb_ = np.ascontiguousarray(np.asarray(L0_attn_b_dwc, f32).reshape(CCH, P).T)
    identw_ = np.eye(P, dtype=ml_dtypes.bfloat16)

    in_maps = []
    for i in range(NCORES):
        s = i * per
        hl = x[s - 1] if i > 0 else np.zeros(C, f32)
        hr = x[s + per] if i < NCORES - 1 else np.zeros(C, f32)
        in_maps.append({
            "xin": np.ascontiguousarray(x[s:s + per]),
            "xhalo": np.ascontiguousarray(np.stack([hl, hr])),
            "hmask": np.array([[1.0 if i > 0 else 0.0],
                               [1.0 if i < NCORES - 1 else 0.0]], f32),
            "wqkvT": wqkvT_, "wprojT": wprojT_, "gate9T": gate9T_,
            "bprojr": bprojr_, "wguT": wguT_, "wdnT": wdnT_,
            "dwcw": dwcw_, "dwcb": dwcb_, "identw": identw_,
        })
    return in_maps, x.shape


def kernel(**inputs):
    in_maps, (ntok, _) = _prep_inputs(**inputs)
    tch = ntok // NCORES // P
    nc = build_nc(tch=tch)
    res = run_bass_kernel_spmd(nc, in_maps, core_ids=list(range(NCORES)))
    out = np.concatenate([r["y"] for r in res.results], axis=0)
    return out.reshape(np.asarray(inputs["L0_attn_x"]).shape)


# version 12
# speedup vs baseline: 1.0630x; 1.0445x over previous
import numpy as np
import ml_dtypes
import concourse.bass as bass
import concourse.mybir as mybir
from concourse.bass_utils import run_bass_kernel_spmd
from contextlib import ExitStack

F32 = mybir.dt.float32
BF16 = mybir.dt.bfloat16
AF = mybir.ActivationFunctionType
ALU = mybir.AluOpType
AX = mybir.AxisListType

NCORES = 8
P = 128
C = 1024
CCH = 8
H = 16
D = 64
OQKV = 3072
ICH = 11
NE = 10
EPS_A = 1e-6
EPS_LN = 1e-5
GU_GROUPS = [(0, 6), (6, 12), (12, 18), (18, 22)]
WDN_HALVES = [(0, 6), (6, 11)]


class TB:
    COMPUTE = ("pe", "act", "dve", "pool", "sp")

    def __init__(self, nc, ctx):
        self.nc = nc
        self.eng = {"pe": nc.tensor, "act": nc.scalar, "dve": nc.vector,
                    "pool": nc.gpsimd, "sp": nc.sync}
        self.sem = {}
        self.cnt = {}
        for name in self.COMPUTE:
            self.sem[name] = ctx.enter_context(nc.semaphore(f"S_{name}"))
            self.cnt[name] = 0
        self.dcnt = {}
        self.last_w = {}
        self.readers = {}
        self.max_waited = {name: {} for name in self.COMPUTE}
        self.pos = {name: 0 for name in self.COMPUTE}
        self.mat_pos = {name: [] for name in self.COMPUTE}
        self.mat_ent = {name: [] for name in self.COMPUTE}
        self.dma_ents = {}

    def dma_sem(self, name, ctx):
        assert name not in self.sem
        self.sem[name] = ctx.enter_context(self.nc.semaphore(name))
        self.dcnt[name] = 0
        self.dma_ents[name] = []

    def _mat(self, ent):
        if ent["val"] is not None:
            return ent["sem"], ent["val"]
        en = ent["eng"]
        mp = self.mat_pos[en]
        p = ent["pos"]
        if mp and p < mp[-1]:
            import bisect
            i = bisect.bisect_left(mp, p)
            sub = self.mat_ent[en][i]
            return sub["sem"], sub["val"]
        self.cnt[en] += 1
        ent["val"] = self.cnt[en]
        ent["sem"] = en
        ent["inst"].then_inc(self.sem[en], 1)
        mp.append(p)
        self.mat_ent[en].append(ent)
        return en, ent["val"]

    def _deps(self, eng_name, reads, writes):
        deps = {}

        def add(ent):
            s, v = self._mat(ent)
            if deps.get(s, 0) < v:
                deps[s] = v

        for k in reads:
            ent = self.last_w.get(k)
            if ent is not None:
                add(ent)
        for k in writes:
            ent = self.last_w.get(k)
            if ent is not None:
                add(ent)
            for r in self.readers.get(k, ()):
                add(r)
        mw = self.max_waited[eng_name]
        return [(s, v) for s, v in deps.items() if mw.get(s, 0) < v]

    def _register(self, ent, reads, writes):
        for k in writes:
            self.last_w[k] = ent
            self.readers[k] = []
        for k in reads:
            self.readers.setdefault(k, []).append(ent)

    def _emit(self, eng_name, fn, reads, writes, dma_sem=None):
        deps = self._deps(eng_name, reads, writes)
        engobj = self.eng[eng_name]
        mw = self.max_waited[eng_name]
        for s, v in deps[:-1]:
            engobj.wait_ge(self.sem[s], v)
            mw[s] = max(mw.get(s, 0), v)
        inst = fn(engobj)
        if deps:
            s, v = deps[-1]
            inst._wait_ge(self.sem[s], v)
            mw[s] = max(mw.get(s, 0), v)
        if dma_sem is None:
            self.pos[eng_name] += 1
            ent = {"eng": eng_name, "inst": inst, "sem": None, "val": None,
                   "pos": self.pos[eng_name]}
        else:
            self.dcnt[dma_sem] += 16
            inst.then_inc(self.sem[dma_sem], 16)
            ent = {"eng": eng_name, "inst": inst, "sem": dma_sem,
                   "val": self.dcnt[dma_sem]}
            self.dma_ents[dma_sem].append(ent)
        self._register(ent, reads, writes)
        return ent

    def emit(self, eng_name, fn, reads=(), writes=()):
        return self._emit(eng_name, fn, list(reads), list(writes))

    def dma(self, q, semname, fn, reads=(), writes=()):
        return self._emit(q, fn, list(reads), list(writes), dma_sem=semname)

    def alias_key(self, new, old):
        if old in self.last_w:
            self.last_w[new] = self.last_w[old]
        self.readers[new] = list(self.readers.get(old, ()))

    def seal_dma_group(self, semname):
        total = self.dcnt[semname]
        for ent in self.dma_ents[semname]:
            ent["val"] = total


def _span_keys(prefix, off, nbytes, blk=2048):
    return [f"{prefix}{b}" for b in range(off // blk, (off + nbytes - 1) // blk + 1)]


def build_nc(tch=16, dbg=False):
    union = (tch == 16)
    TOK = tch * P
    TW = TOK + 2
    TH = TOK // 2
    nhalf = tch // 2
    QP = [(0, "q"), (1, "q"), (2, "k"), (3, "k"), (4, "v"), (5, "v")]
    R6B = 22528
    O_LN1 = [0, 2048]
    O_QC = [4096, 6144]
    O_KC3 = 8192
    O_KN = 10240
    O_S1 = [12288, 14336]
    O_S2 = [16384, 17408]
    O_VE = 18432
    O_CU = 0
    O_AOT = [4 * TOK, 4 * TOK + 2048]
    O_LN2 = [4 * TOK + 4096, 4 * TOK + 6144]
    O_RT = 4 * TOK + 8192
    assert O_RT + 64 * tch + 2048 <= R6B + 2048

    nc = bass.Bass("TRN2", target_bir_lowering=False)

    xin = nc.dram_tensor("xin", [TOK, C], F32, kind="ExternalInput")
    xhalo = nc.dram_tensor("xhalo", [2, C], F32, kind="ExternalInput")
    hmask = nc.dram_tensor("hmask", [2, 1], F32, kind="ExternalInput")
    wqkvP = nc.dram_tensor("wqkvP", [OQKV // P, P, C], F32, kind="ExternalInput")
    wprojT = nc.dram_tensor("wprojT", [C, C], F32, kind="ExternalInput")
    gate9P = nc.dram_tensor("gate9P", [P, CCH, 9], F32, kind="ExternalInput")
    bprojr = nc.dram_tensor("bprojr", [1, C], F32, kind="ExternalInput")
    wguP = nc.dram_tensor("wguP", [NE, 2 * ICH, P, C], F32, kind="ExternalInput")
    wdnT = nc.dram_tensor("wdnT", [NE, 1408, C], F32, kind="ExternalInput")
    dwcw = nc.dram_tensor("dwcw", [P, CCH, 3], F32, kind="ExternalInput")
    dwcb = nc.dram_tensor("dwcb", [P, CCH], F32, kind="ExternalInput")
    identw = nc.dram_tensor("identw", [P, P], BF16, kind="ExternalInput")
    y = nc.dram_tensor("y", [TOK, C], F32, kind="ExternalOutput")
    dbgx = nc.dram_tensor("dbgx", [TOK, C], F32, kind="ExternalOutput") if dbg else None
    dbgkv = nc.dram_tensor("dbgkv", [D, H * 65], F32, kind="ExternalOutput") if dbg else None
    dbgkn = nc.dram_tensor("dbgkn", [P, C], BF16, kind="ExternalOutput") if dbg else None
    dbgve = nc.dram_tensor("dbgve", [P, H * 65], BF16, kind="ExternalOutput") if dbg else None
    dbgqc = nc.dram_tensor("dbgqc", [P, C], BF16, kind="ExternalOutput") if dbg else None
    wguH = nc.dram_tensor("wguH", [NE, 2 * ICH, P, C], BF16)
    wdnH = nc.dram_tensor("wdnH", [NE, ICH, P, C], BF16)
    cc_in = nc.dram_tensor("cc_in", [D, H * 65], F32)
    cc_out = nc.dram_tensor("cc_out", [D, H * 65], F32)

    with ExitStack() as ctx:
        e = ctx.enter_context
        tb = TB(nc, ctx)
        for s in ("dx0", "dx1", "dx2", "dx3", "dxr", "dwq", "dsm", "dsw",
                  "dgu0", "dgu1", "dgu2", "dgu3", "ddn0", "ddn1", "dcc", "dy", "ddbg", "dstA0", "dstA1", "dstg0", "dstg1", "dg9", "dbp", "dpo0", "dpo1", "dbl0", "dbl1", "dbl2", "dbl3", "dbl4"):
            tb.dma_sem(s, ctx)

        rx = e(nc.sbuf_tensor("rx", [P, tch * C], F32))
        if union:
            wqkv_bf = rx[:, 0:12 * C].bitcast(BF16).rearrange(
                "p (j o) -> p j o", o=OQKV)
            wq_keys = lambda j: _span_keys("Rx", j * OQKV * 2, OQKV * 2, 4096)
        else:
            wqkv_t = e(nc.sbuf_tensor("wqkv_t", [P, CCH, OQKV], BF16))
            wqkv_bf = wqkv_t[:, :, :]
            wq_keys = lambda j: [f"wq{j}"]

        xv = lambda t: rx[:, t * C:(t + 1) * C]
        ring_slot = lambda t: (12 + t % 4) if union else t
        xk = lambda t: f"Rx{t}"

        r3 = e(nc.sbuf_tensor("r3", [P, CCH, TW], BF16))
        r4t = e(nc.sbuf_tensor("r4t", [P, 2 * ICH, CCH, P], BF16))
        r5 = e(nc.sbuf_tensor("r5", [P, ICH, C], BF16))
        r6 = e(nc.sbuf_tensor("r6", [P, R6B // 2], BF16))

        def r6v(off, dtype, shape):
            n = int(np.prod(shape))
            nbytes = n * mybir.dt.size(dtype)
            ap = r6[:, off // 2: (off + nbytes) // 2]
            if dtype != BF16:
                ap = ap.bitcast(dtype)
            if len(shape) > 1:
                ap = ap.rearrange("p (a b) -> p a b", b=shape[-1])
            return ap

        r6k = lambda off, nbytes: _span_keys("R6", off, nbytes)

        def r4k(c0, c1):
            return [f"R4_{k}" for k in range(c0 // P, (c1 - 1) // P + 1)]

        ident = e(nc.sbuf_tensor("ident", [P, P], BF16))
        bnst = e(nc.sbuf_tensor("bnst", [P, 2, 12], F32))
        mv = e(nc.sbuf_tensor("mv", [P, 2], F32))
        mstat = e(nc.sbuf_tensor("mstat", [P, tch + 1, 2], F32))
        sk2 = e(nc.sbuf_tensor("sk2", [P, H], F32))
        kv_sb = e(nc.sbuf_tensor("kv_sb", [P, H * 65], F32))
        kvg_bf = e(nc.sbuf_tensor("kvg_bf", [P, H, 65], BF16))
        g_all = e(nc.sbuf_tensor("g_all", [P, 2, H], F32))
        w10 = e(nc.sbuf_tensor("w10", [P, tch, 10], F32))
        bproj_bf = e(nc.sbuf_tensor("bproj_bf", [1, C], BF16))
        ones_c = e(nc.sbuf_tensor("ones_c", [1, P], BF16))
        gate9_bf = e(nc.sbuf_tensor("gate9_bf", [P, CCH, 9], BF16))
        gate9f = e(nc.sbuf_tensor("gate9f", [P, CCH, 9], F32))
        dwcw_sb = e(nc.sbuf_tensor("dwcw_sb", [P, CCH, 3], F32))
        dwcb_sb = e(nc.sbuf_tensor("dwcb_sb", [P, CCH], F32))
        hm_sb = e(nc.sbuf_tensor("hm_sb", [2, 1], F32))
        cst_eps = e(nc.sbuf_tensor("cst_eps", [P, 1], F32))
        xh_sb = e(nc.sbuf_tensor("xh_sb", [P, C], F32))
        ln1h_tok = e(nc.sbuf_tensor("ln1h_tok", [2, C], BF16))
        ln1h_fm = e(nc.sbuf_tensor("ln1h_fm", [P, CCH, 2], BF16))
        qchT = e(nc.sbuf_tensor("qchT", [P, CCH, 2], BF16))
        gh = e(nc.sbuf_tensor("gh", [2, H], F32))
        hstat = e(nc.sbuf_tensor("hstat", [2, 16], F32))
        qch_tok = ln1h_tok
        aoh_tok = ln1h_tok
        lg_all = r6v(O_RT, F32, (tch, 9))
        ex_all = r6v(O_RT + tch * 36, F32, (tch, 8))
        sc_all = r6v(O_RT + tch * 68, F32, (tch, 8))
        mx8 = r6v(O_RT + tch * 100, F32, (tch, 8))
        rtmp = r6v(O_RT + tch * 132, F32, (tch, 2))
        RTN = tch * 140
        k_lg = r6k(O_RT, tch * 36)
        k_ex = r6k(O_RT + tch * 36, tch * 32)
        k_sc = r6k(O_RT + tch * 68, tch * 32)
        k_mx = r6k(O_RT + tch * 100, tch * 32)
        k_rt = r6k(O_RT + tch * 132, tch * 8)

        ps0 = e(nc.psum_tensor("ps0", [P, 512], F32))
        ps1 = e(nc.psum_tensor("ps1", [P, 512], F32))
        ps2 = e(nc.psum_tensor("ps2", [P, 512], F32))
        ps3 = e(nc.psum_tensor("ps3", [P, 512], F32))
        ps45 = e(nc.psum_tensor("ps45", [P, 1024], F32))
        ps6 = e(nc.psum_tensor("ps6", [P, 512], F32))
        ps7 = e(nc.psum_tensor("ps7", [P, 512], F32))
        kvbank = [(ps0, "pb0", 0, 7), (ps1, "pb1", 7, 14), (ps6, "pb6", 14, 16)]

        block = e(nc.Block())

        @block.vector
        def _build(_v):
            V, A, T, G, S = "dve", "act", "pe", "pool", "sp"

            tb.dma(A, "dsm", lambda s: s.dma_start(out=ident[:, :], in_=identw[:, :]),
                   writes=["ident"])
            tb.dma(A, "dsm", lambda s: s.dma_start(out=xh_sb[0:2, :], in_=xhalo[:, :]),
                   writes=["xh"])
            tb.dma(A, "dsm", lambda s: s.dma_start(out=hm_sb[:, :], in_=hmask[:, :]),
                   writes=["hm"])
            tb.dma(A, "dsm", lambda s: s.dma_start(out=dwcw_sb[:, :, :], in_=dwcw[:, :, :]),
                   writes=["dwcw"])
            tb.dma(A, "dsm", lambda s: s.dma_start(out=dwcb_sb[:, :], in_=dwcb[:, :]),
                   writes=["dwcb"])
            def wqkey(j, c0, c1):
                if union:
                    return _span_keys("Rx", j * OQKV * 2 + c0 * 2, (c1 - c0) * 2, 4096)
                return [f"wq{j}"]

            sta_n = [0]

            def tail_slot(sl):
                return r4t[:, 18 + 2 * sl:20 + 2 * sl, :, :].rearrange(
                    "p a b c -> p (a b c)").bitcast(F32)

            def tail_keys(sl):
                return [f"R4_{18 + 2 * sl}", f"R4_{19 + 2 * sl}"]

            def stageA(src_ap, dst_ap, dst_keys):
                sl = sta_n[0] % 2
                sta_n[0] += 1
                slot = tail_slot(sl)
                slk = tail_keys(sl)
                tb.dma(S, f"dstA{sl}", lambda s, src_ap=src_ap, slot=slot:
                       s.dma_start(out=slot, in_=src_ap), writes=slk)
                tb.emit(G, lambda g, slot=slot, dst_ap=dst_ap: g.tensor_copy(
                    dst_ap, slot.rearrange("p (j o) -> p j o", o=P)),
                    reads=slk, writes=list(dst_keys))

            for oc in range(OQKV // P):
                stageA(wqkvP[oc, :, :],
                       wqkv_bf[:, :, oc * P:(oc + 1) * P],
                       [k for j in range(CCH) for k in wqkey(j, oc * P, (oc + 1) * P)])
            for j in range(CCH):
                stageA(wprojT[j * P:(j + 1) * P, :].rearrange(
                           "p (a o) -> p a o", o=P),
                       r5[:, j, :].rearrange("p (a o) -> p a o", o=P),
                       [f"R5_{j}"])
            tb.dma(A, "dg9", lambda s: s.dma_start(
                out=gate9f[:, :, :], in_=gate9P[:, :, :]),
                writes=["gate9f"])
            tb.emit(V, lambda v: v.tensor_copy(gate9_bf[:, :, :], gate9f[:, :, :]),
                    reads=["gate9f"], writes=["gate9"])
            tb.dma(A, "dbp", lambda s: s.dma_start(
                out=kv_sb[0:1, 0:C], in_=bprojr[:, :]), writes=["kvsb"])
            tb.emit(V, lambda v: v.tensor_copy(bproj_bf[:, :], kv_sb[0:1, 0:C]),
                    reads=["kvsb"], writes=["bproj"])
            tb.seal_dma_group("dsm")
            tb.seal_dma_group("dsw")

            def ring_load(t):
                sl = ring_slot(t)
                tb.dma(A, f"dx{t % 4}",
                       lambda s, t=t, sl=sl: s.dma_start(
                           out=rx[:, sl * C:(sl + 1) * C],
                           in_=xin[t * P:(t + 1) * P, :]),
                       writes=[f"Rx{sl}"])
            for t in range(min(4, tch)):
                ring_load(t)
            tb.emit(G, lambda g: g.memset(ones_c[:, :], 1.0), writes=["ones"])
            tb.emit(V, lambda v: v.memset(cst_eps[:, :], EPS_LN), writes=["csteps"])
            tb.emit(V, lambda v: v.memset(kv_sb[0:D, :], 0.0), writes=["kvsb"])
            tb.emit(V, lambda v: v.memset(
                r6v(O_VE, BF16, (H, 65))[:, :, 64:65], 1.0),
                writes=r6k(O_VE, 2080))
            tb.emit(V, lambda v: v.tensor_scalar(
                dwcw_sb[:, :, 1:2], dwcw_sb[:, :, 1:2], 1.0, None, ALU.add),
                reads=["dwcw"], writes=["dwcw"])

            QDIV = 4 if tch % 4 == 0 else 2
            TQ = TOK // QDIV
            nq = tch // QDIV
            STG = [ICH * TQ * 2, ICH * TQ * 2 + 4096]
            assert STG[1] + 4096 <= R6B
            sgbuf = kv_sb[:, :].bitcast(F32)
            pp_n = [0]

            def prepass_piece(src_ap, dst_ap, dst_key, early):
                sl = pp_n[0] % 2
                pp_n[0] += 1
                if early:
                    f32s = tail_slot(sl)
                    f32k = tail_keys(sl)
                    dsem_i = f"dstA{sl}"
                else:
                    f32s = r6v(STG[sl], F32, (1024,))
                    f32k = r6k(STG[sl], 4096)
                    dsem_i = f"dstg{sl}"
                bfs = xh_sb[:, sl * 512:sl * 512 + 512].bitcast(BF16)
                bfk = [f"xh{sl}"]
                dsem_o = f"dpo{sl}"
                tb.dma(S, dsem_i, lambda s, src_ap=src_ap, f32s=f32s:
                       s.dma_start(out=f32s, in_=src_ap), writes=f32k)
                tb.emit(V, lambda v, f32s=f32s, bfs=bfs: v.tensor_copy(bfs, f32s),
                        reads=f32k, writes=bfk)
                okey = f"{dst_key}_p{pp_n[0]}"
                tb.dma(S, dsem_o, lambda s, bfs=bfs, dst_ap=dst_ap:
                       s.dma_start(out=dst_ap, in_=bfs), reads=bfk,
                       writes=[okey])
                return okey

            ppkeys = {}

            def prepass_args(ee):
                args = []
                for oc in range(2 * ICH):
                    args.append(("gu", ee, wguP[ee, oc, :, :], wguH[ee, oc, :, :]))
                for m in range(ICH):
                    args.append(("dn", ee,
                                 wdnT[ee, m * P:(m + 1) * P, :], wdnH[ee, m, :, :]))
                return args

            def emit_prepass(args, early):
                for kind, ee, src_ap, dst_ap in args:
                    ppkeys.setdefault(ee, ([], []))
                    k = prepass_piece(src_ap, dst_ap,
                                      f"wguH{ee}" if kind == "gu" else f"wdnH{ee}",
                                      early)
                    ppkeys[ee][0 if kind == "gu" else 1].append(k)

            def prepass_expert(ee, early=False):
                emit_prepass(prepass_args(ee), early)

            WGU_SLABS = [(0, 6), (6, 12), (12, 22)]

            def bf16_load(ee):
                for si, (c0, c1) in enumerate(WGU_SLABS):
                    tb.dma(A, f"dbl{si}", lambda s, ee=ee, c0=c0, c1=c1:
                           s.dma_start(
                               out=r4t[:, c0:c1, :, :],
                               in_=wguH[ee, c0:c1, :, :].rearrange("k p x -> p k x")),
                           reads=ppkeys[ee][0],
                           writes=[f"R4_{k}" for k in range(c0, c1)])
                for si, (m0, m1) in enumerate(WDN_HALVES):
                    tb.dma(A, f"dbl{3 + si}", lambda s, ee=ee, m0=m0, m1=m1:
                           s.dma_start(
                               out=r5[:, m0:m1, :],
                               in_=wdnH[ee, m0:m1, :, :].rearrange("m p x -> p m x")),
                           reads=ppkeys[ee][1],
                           writes=[f"R5_{m}" for m in range(m0, m1)])

            def pair_order(ee):
                return list(range(ICH))


            def emit_ln(x_ap, x_keys, pp, stat_idx, out_ap, out_keys, par):
                if pp == P:
                    bn, bnk = bnst[:, par, :], f"bnst{par}"
                    ag, agk = mv[:, :], "mv"
                    ms, msk = mstat[:, stat_idx, :], f"mstat{stat_idx}"
                else:
                    bn, bnk = hstat[0:2, 0:12], "hstat"
                    ag, agk = hstat[0:2, 12:14], "hstat1"
                    ms, msk = hstat[0:2, 14:16], "hstat2"
                tb.emit(V, lambda v: v.bn_stats(bn[:, 0:6], x_ap[:, 0:512]),
                        reads=x_keys, writes=[bnk])
                tb.emit(V, lambda v: v.bn_stats(bn[:, 6:12], x_ap[:, 512:1024]),
                        reads=[bnk] + x_keys, writes=[bnk])
                tb.emit(V, lambda v: v.bn_aggr(ag, bn), reads=[bnk], writes=[agk])
                tb.emit(V, lambda v: v.tensor_copy(ms[:, 0:1], ag[:, 0:1]),
                        reads=[agk], writes=[msk])
                tb.emit(A, lambda a: a.activation(ms[:, 1:2], ag[:, 1:2], AF.Sqrt,
                                                  bias=cst_eps[0:pp, 0:1]),
                        reads=[agk, "csteps"], writes=[msk])
                tb.emit(V, lambda v: v.reciprocal(ms[:, 1:2], ms[:, 1:2]),
                        reads=[msk], writes=[msk])
                tb.emit(V, lambda v: v.tensor_scalar(
                    out_ap, x_ap, ms[:, 0:1], ms[:, 1:2], ALU.subtract, ALU.mult),
                    reads=x_keys + [msk], writes=out_keys)

            def emit_tr8(src_ap, src_keys, dst_ap, dst_keys, pp):
                for j in range(CCH):
                    tb.emit(T, lambda t, j=j: t.matmul(
                        ps45[:, j * pp:(j + 1) * pp],
                        src_ap[:, j * P:(j + 1) * P], ident[0:pp, 0:pp],
                        start=True, stop=True),
                        reads=list(src_keys) + ["ident"],
                        writes=(["pb4", "pb5"] if j in (0, CCH - 1) else []))
                tb.emit(A, lambda a: a.activation(
                    dst_ap, ps45[:, 0:CCH * pp].rearrange("p (j q) -> p j q", q=pp),
                    AF.Copy),
                    reads=["pb4", "pb5"], writes=list(dst_keys))

            def phase_a_chunk(t, halo):
                par = t % 2
                if halo:
                    pp = 2
                    xap, xkeys = xh_sb[0:2, :], ["xh"]
                    ln1, ln1k = ln1h_tok[0:2, :], ["ln1h"]
                    pieces = QP[0:2]
                else:
                    pp = P
                    sl = ring_slot(t)
                    xap, xkeys = rx[:, sl * C:(sl + 1) * C], [f"Rx{sl}"]
                    ln1 = r6v(O_LN1[par], BF16, (C,))
                    ln1k = r6k(O_LN1[par], 2048)
                    pieces = QP
                emit_ln(xap, xkeys, pp, tch if halo else t, ln1, ln1k, par)
                if halo:
                    emit_tr8(ln1, ln1k, ln1h_fm[:, :, :], ["ln1hfm"], 2)
                else:
                    col = 1 + t * P
                    emit_tr8(ln1, ln1k, r3[:, :, col:col + P],
                             [f"R3_{j}" for j in range(CCH)], P)
                if halo:
                    qc, qck = qch_tok, ["ln1h"]
                else:
                    qc = r6v(O_QC[par], BF16, (C,))
                    qck = r6k(O_QC[par], 2048)
                kc3 = r6v(O_KC3, BF16, (C,))
                kc3k = r6k(O_KC3, 2048)
                kn = r6v(O_KN, BF16, (C,))
                knk = r6k(O_KN, 2048)
                ve = r6v(O_VE, BF16, (H, 65))
                vek = r6k(O_VE, 2080)

                for pc, kind in pieces:
                    if halo:
                        pbank, pbk = ps7, "pb7"
                    else:
                        pbank, pbk = (ps2, "pb2") if pc % 2 == 0 else (ps3, "pb3")
                    out_ps = pbank[0:pp, :]
                    for j in range(CCH):
                        if halo:
                            lhsT, lk = ln1h_fm[:, j, 0:2], ["ln1hfm"]
                        else:
                            lhsT, lk = r3[:, j, 1 + t * P:1 + (t + 1) * P], [f"R3_{j}"]
                        tb.emit(T, lambda tt, j=j, pc=pc, lhsT=lhsT, out_ps=out_ps:
                                tt.matmul(out_ps, lhsT,
                                          wqkv_bf[:, j, pc * 512:(pc + 1) * 512],
                                          start=(j == 0), stop=(j == CCH - 1)),
                                reads=lk + wqkey(j, pc * 512, (pc + 1) * 512),
                                writes=([pbk] if j in (0, CCH - 1) else []))
                    spar = pc % 2
                    s1 = r6v(O_S1[spar], F32, (512,))[0:pp, :]
                    s1k = r6k(O_S1[spar], 2048)
                    s2 = r6v(O_S2[spar], BF16, (512,))[0:pp, :]
                    s2k = r6k(O_S2[spar], 1024)
                    off = (pc % 2) * 512
                    if kind == "q":
                        dst, dstk = qc[0:pp, off:off + 512], qck
                    elif kind == "k":
                        dst, dstk = kc3[0:pp, off:off + 512], kc3k
                    if kind in ("q", "k"):
                        tb.emit(V, lambda v, s1=s1, out_ps=out_ps: v.tensor_scalar(
                            s1, out_ps, 0.0, EPS_A, ALU.max, ALU.add),
                            reads=[pbk], writes=s1k)
                        tb.emit(A, lambda a, s1=s1, s2=s2: a.activation(
                            s2, s1, AF.Square), reads=s1k, writes=s2k)
                        tb.emit(G, lambda g, s1=s1, s2=s2, dst=dst: g.tensor_tensor(
                            dst, s2, s1, op=ALU.mult),
                            reads=s1k + s2k, writes=dstk)
                    else:
                        tb.emit(A, lambda a, out_ps=out_ps, pc=pc: a.activation(
                            ve[:, (pc - 4) * 8:(pc - 4) * 8 + 8, 0:64],
                            out_ps.rearrange("p (h d) -> p h d", d=64), AF.Copy),
                            reads=[pbk], writes=vek)
                if halo:
                    return
                tb.emit(V, lambda v: v.tensor_reduce(
                    sk2[:, :].unsqueeze(2),
                    kc3.rearrange("p (h d) -> p h d", d=64),
                    axis=AX.X, op=ALU.add),
                    reads=kc3k, writes=["sk"])
                tb.emit(V, lambda v: v.reciprocal(sk2[:, :], sk2[:, :]),
                        reads=["sk"], writes=["sk"])
                tb.emit(G, lambda g: g.tensor_tensor(
                    kn.rearrange("p (h d) -> p h d", d=64),
                    kc3.rearrange("p (h d) -> p h d", d=64),
                    sk2[:, :].unsqueeze(2).to_broadcast([P, H, 64]), op=ALU.mult),
                    reads=kc3k + ["sk"], writes=knk)
                for h in range(H):
                    bank, bkey, lo, hi = kvbank[h // 7 if h < 14 else 2]
                    off = (h - lo) * 65
                    tb.emit(T, lambda tt, h=h, bank=bank, off=off: tt.matmul(
                                bank[0:D, off:off + 65],
                                kn[:, h * D:(h + 1) * D], ve[:, h, :],
                                start=True, stop=True),
                            reads=knk + vek, writes=[bkey])
                for bank, bkey, lo, hi in kvbank:
                    tb.emit(V, lambda v, bank=bank, lo=lo, hi=hi: v.tensor_tensor(
                        kv_sb[0:D, lo * 65:hi * 65], kv_sb[0:D, lo * 65:hi * 65],
                        bank[0:D, 0:(hi - lo) * 65], op=ALU.add),
                        reads=[bkey, "kvsb"], writes=["kvsb"])
                if dbg and t == 5:
                    tb.dma(S, "ddbg", lambda s: s.dma_start(out=dbgkn[:, :], in_=kn),
                           reads=knk)
                    tb.dma(S, "ddbg", lambda s: s.dma_start(
                        out=dbgve[:, :], in_=ve.rearrange("p a b -> p (a b)")),
                        reads=vek)
                    tb.dma(S, "ddbg", lambda s: s.dma_start(out=dbgqc[:, :], in_=qc),
                           reads=qck)
                    tb.seal_dma_group("ddbg")
                emit_tr8(qc, qck, r4t[:, t, :, :], [f"R4_{t}"], P)

            phase_a_chunk(0, halo=True)
            emit_tr8(qch_tok, ["ln1h"], qchT[:, :, :], ["qchT"], 2)
            tb.alias_key("xh0", "xh")
            tb.alias_key("xh1", "xh")
            early_args = prepass_args(0) + prepass_args(1)
            per_t = (len(early_args) + tch - 1) // tch
            for t in range(tch):
                phase_a_chunk(t, halo=False)
                if t + 4 < tch:
                    ring_load(t + 4)
                emit_prepass(early_args[t * per_t:(t + 1) * per_t], early=True)

            tb.dma(A, "dcc", lambda g: g.dma_start(out=cc_in[:, :], in_=kv_sb[0:D, :]),
                   reads=["kvsb"], writes=["ccin"])
            tb.emit(G, lambda g: g.collective_compute(
                "AllReduce", ALU.add, replica_groups=[list(range(NCORES))],
                ins=[cc_in[:, :].opt()], outs=[cc_out[:, :].opt()]),
                reads=["ccin"], writes=["ccout"])
            tb.dma(A, "dcc", lambda g: g.dma_start(out=kv_sb[0:D, :], in_=cc_out[:, :]),
                   reads=["ccout"], writes=["kvsb"])
            tb.dma(A, "dcc", lambda g: g.dma_start(out=kv_sb[D:P, :], in_=cc_out[:, :]),
                   reads=["ccout"], writes=["kvsb"])
            kvv = kv_sb[:, :].rearrange("p (h e) -> p h e", e=65)
            tb.emit(V, lambda v: v.tensor_scalar(
                kvv[:, :, 64:65], kvv[:, :, 64:65], EPS_A, None, ALU.add),
                reads=["kvsb"], writes=["kvsb"])
            tb.emit(V, lambda v: v.tensor_copy(kvg_bf[:, :, :], kvv),
                    reads=["kvsb"], writes=["kvg"])
            if dbg:
                tb.dma(S, "ddbg", lambda s: s.dma_start(
                    out=dbgkv[:, :], in_=kv_sb[0:D, :]), reads=["kvsb"])

            if union:
                for t in range(12):
                    tb.dma(A, "dxr", lambda s, t=t: s.dma_start(
                        out=xv(t), in_=xin[t * P:(t + 1) * P, :]),
                        writes=[xk(t)])
                tb.seal_dma_group("dxr")

            def emit_attn(t, halo):
                pp = 2 if halo else P
                par = 0 if halo else t % 2
                if halo:
                    ao, aok = aoh_tok[0:2, :], ["ln1h"]
                    gg, ggk = gh[0:2, :], "gh"
                else:
                    ao = r6v(O_AOT[par], BF16, (C,))
                    aok = r6k(O_AOT[par], 2048)
                    gg, ggk = g_all[:, par, :], f"gall{par}"
                absets = [kvbank,
                          [(ps2, "pb2", 0, 7), (ps3, "pb3", 7, 14),
                           (ps7, "pb7", 14, 16)]]
                for bank, bkey, lo, hi in absets[0 if halo else t % 2]:
                    nh = hi - lo
                    for h in range(lo, hi):
                        if halo:
                            lhsT = qchT[(h % 2) * D:(h % 2) * D + D, h // 2, 0:2]
                            lk = ["qchT"]
                        else:
                            lhsT = r4t[(h % 2) * D:(h % 2) * D + D, t, h // 2, :]
                            lk = [f"R4_{t}"]
                        tb.emit(T, lambda tt, bank=bank, h=h, lo=lo, hi=hi,
                                lhsT=lhsT: tt.matmul(
                                    bank[0:pp, (h - lo) * 65:(h - lo) * 65 + 65],
                                    lhsT, kvg_bf[(h % 2) * D:(h % 2) * D + D, h, :],
                                    start=True, stop=True),
                                reads=lk + ["kvg"], writes=[bkey])
                    bview = bank[0:pp, 0:nh * 65].rearrange("p (h e) -> p h e", e=65)
                    tb.emit(V, lambda v, bview=bview, lo=lo, hi=hi: v.reciprocal(
                        gg[0:pp, lo:hi], bview[:, :, 64:65].squeeze(2)),
                        reads=[bkey], writes=[ggk])
                    tb.emit(V, lambda v, bview=bview, lo=lo, hi=hi, nh=nh:
                            v.tensor_tensor(
                                ao[0:pp, lo * D:hi * D].rearrange(
                                    "p (h d) -> p h d", d=D),
                                bview[:, :, 0:64],
                                gg[0:pp, lo:hi].unsqueeze(2).to_broadcast(
                                    [pp, nh, D]),
                                op=ALU.mult),
                            reads=[bkey, ggk], writes=aok)
                if halo:
                    tb.emit(V, lambda v: v.tensor_scalar(
                        ao[0:2, :], ao[0:2, :], hm_sb[0:2, 0:1], None, ALU.mult),
                        reads=aok + ["hm"], writes=aok)
                    for j in range(CCH):
                        tb.emit(T, lambda tt, j=j: tt.matmul(
                            ps45[:, j * 2:(j + 1) * 2], ao[0:2, j * P:(j + 1) * P],
                            ident[0:2, 0:2], start=True, stop=True),
                            reads=aok + ["ident"],
                            writes=(["pb4", "pb5"] if j in (0, CCH - 1) else []))
                    pv = ps45[:, 0:2 * CCH].rearrange("p (j q) -> p j q", q=2)
                    tb.emit(A, lambda a: a.activation(
                        r3[:, :, 0:1], pv[:, :, 0:1], AF.Copy),
                        reads=["pb4", "pb5"],
                        writes=[f"R3_{j}" for j in range(CCH)])
                    tb.emit(A, lambda a: a.activation(
                        r3[:, :, TW - 1:TW], pv[:, :, 1:2], AF.Copy),
                        reads=["pb4", "pb5"],
                        writes=[f"R3_{j}" for j in range(CCH)])
                else:
                    col = 1 + t * P
                    emit_tr8(ao, aok, r3[:, :, col:col + P],
                             [f"R3_{j}" for j in range(CCH)], P)

            for t in range(tch):
                emit_attn(t, halo=False)
            emit_attn(0, halo=True)

            for j in range(CCH):
                u = r6v(O_CU, F32, (TOK,))
                uk = r6k(O_CU, 4 * TOK)
                tb.emit(V, lambda v, j=j, u=u: v.tensor_scalar(
                    u, r3[:, j, 2:2 + TOK], dwcw_sb[:, j, 2:3], dwcb_sb[:, j:j + 1],
                    ALU.mult, ALU.add),
                    reads=[f"R3_{j}", "dwcw", "dwcb"], writes=uk)
                tb.emit(V, lambda v, j=j, u=u: v.scalar_tensor_tensor(
                    u, r3[:, j, 0:TOK], dwcw_sb[:, j, 0:1], u, ALU.mult, ALU.add),
                    reads=[f"R3_{j}", "dwcw"] + uk, writes=uk)
                tb.emit(V, lambda v, j=j, u=u: v.scalar_tensor_tensor(
                    r4t[:, 0:tch, j, :],
                    r3[:, j, 1:1 + TOK].rearrange("p (k o) -> p k o", o=P),
                    dwcw_sb[:, j, 1:2],
                    u.rearrange("p (k o) -> p k o", o=P),
                    ALU.mult, ALU.add),
                    reads=[f"R3_{j}", "dwcw"] + uk, writes=r4k(0, TOK))

            def emit_proj(t):
                for piece in range(2):
                    bank, bkey = (ps2, "pb2") if piece == 0 else (ps3, "pb3")
                    for j in range(CCH):
                        tb.emit(T, lambda tt, j=j, piece=piece, t=t, bank=bank:
                                tt.matmul(bank[:, :],
                                          r4t[:, t, j, :],
                                          r5[:, j, piece * 512:(piece + 1) * 512],
                                          start=(j == 0), stop=False),
                                reads=[f"R4_{t}", f"R5_{j}"],
                                writes=([bkey] if j == 0 else []))
                    tb.emit(T, lambda tt, piece=piece, bank=bank: tt.matmul(
                        bank[:, :], ones_c[0:1, :],
                        bproj_bf[0:1, piece * 512:(piece + 1) * 512],
                        start=False, stop=True),
                        reads=["ones", "bproj"], writes=[bkey])
                    tb.emit(V, lambda v, t=t, piece=piece, bank=bank: v.tensor_tensor(
                        xv(t)[:, piece * 512:(piece + 1) * 512],
                        xv(t)[:, piece * 512:(piece + 1) * 512],
                        bank[:, :], op=ALU.add),
                        reads=[xk(t), bkey], writes=[xk(t)])

            def emit_ln2chain(t):
                par = t % 2
                ln2 = r6v(O_LN2[par], BF16, (C,))
                ln2k = r6k(O_LN2[par], 2048)
                emit_ln(xv(t), [xk(t)], P, t, ln2, ln2k, par)

            def emit_trgate(t):
                par = t % 2
                ln2 = r6v(O_LN2[par], BF16, (C,))
                ln2k = r6k(O_LN2[par], 2048)
                col = t * P
                emit_tr8(ln2, ln2k, r3[:, :, col:col + P],
                         [f"R3_{j}" for j in range(CCH)], P)
                for j in range(CCH):
                    tb.emit(T, lambda tt, j=j, t=t: tt.matmul(
                        ps7[:, 0:9], r3[:, j, t * P:(t + 1) * P], gate9_bf[:, j, :],
                        start=(j == 0), stop=(j == CCH - 1)),
                        reads=[f"R3_{j}", "gate9"],
                        writes=(["pb7"] if j in (0, CCH - 1) else []))
                tb.emit(A, lambda a, t=t: a.activation(
                    lg_all[:, t, :], ps7[:, 0:9], AF.Copy),
                    reads=["pb7"], writes=k_lg)

            for t in range(tch):
                emit_proj(t)
                emit_ln2chain(t)
                if t > 0:
                    emit_trgate(t - 1)
            emit_trgate(tch - 1)

            if dbg:
                for t in range(tch):
                    tb.dma(S, "ddbg", lambda s, t=t: s.dma_start(
                        out=dbgx[t * P:(t + 1) * P, :], in_=xv(t)),
                        reads=[xk(t)])
                tb.seal_dma_group("ddbg")

            tb.emit(V, lambda v: v.tensor_reduce(
                rtmp[:, :, 0:1], lg_all[:, :, 0:8], axis=AX.X, op=ALU.max,
                negate=True),
                reads=k_lg, writes=k_rt)
            for t in range(tch):
                tb.emit(A, lambda a, t=t: a.activation(
                    ex_all[:, t, :], lg_all[:, t, 0:8], AF.Exp,
                    bias=rtmp[:, t, 0:1]),
                    reads=k_lg + k_rt, writes=k_ex)
            tb.emit(V, lambda v: v.tensor_reduce(
                rtmp[:, :, 1:2], ex_all[:, :, :], axis=AX.X, op=ALU.add),
                reads=k_ex, writes=k_rt)
            tb.emit(V, lambda v: v.reciprocal(rtmp[:, :, 1:2], rtmp[:, :, 1:2]),
                    reads=k_rt, writes=k_rt)
            tb.emit(V, lambda v: v.tensor_tensor(
                sc_all[:, :, :], ex_all[:, :, :],
                rtmp[:, :, 1:2].to_broadcast([P, tch, 8]), op=ALU.mult),
                reads=k_ex + k_rt, writes=k_sc)
            for t in range(tch):
                tb.emit(V, lambda v, t=t: v.max(out=mx8[:, t, :],
                                                in_=sc_all[:, t, :]),
                        reads=k_sc, writes=k_mx)
            tb.emit(V, lambda v: v.tensor_tensor(
                ex_all[:, :, :], sc_all[:, :, :],
                mx8[:, :, 1:2].to_broadcast([P, tch, 8]), op=ALU.is_ge),
                reads=k_sc + k_mx, writes=k_ex)
            tb.emit(V, lambda v: v.tensor_tensor(
                w10[:, :, 0:8], sc_all[:, :, :], ex_all[:, :, :], op=ALU.mult),
                reads=k_sc + k_ex, writes=["w10"])
            tb.emit(A, lambda a: a.activation(
                w10[:, :, 8:9], lg_all[:, :, 8:9], AF.Sigmoid),
                reads=k_lg, writes=["w10"])
            tb.emit(V, lambda v: v.tensor_copy(w10[:, :, 9:10], w10[:, :, 8:9]),
                    reads=["w10"], writes=["w10"])

            tb.alias_key("kvsb0", "kvsb")
            tb.alias_key("kvsb1", "kvsb")
            bf16_load(0)
            for ee in range(NE):
                po = pair_order(ee)
                for qr in range(QDIV):
                    base = qr * TQ
                    for mi, m in enumerate(po):
                        gb, gk = (ps0, "pb0") if mi % 2 == 0 else (ps2, "pb2")
                        ub, uk_ = (ps1, "pb1") if mi % 2 == 0 else (ps3, "pb3")
                        hrow = r6v(m * TQ * 2, BF16, (TQ,))
                        hk = r6k(m * TQ * 2, TQ * 2)
                        for bank, bky, oc in ((gb, gk, 2 * m), (ub, uk_, 2 * m + 1)):
                            for j in range(CCH):
                                tb.emit(T, lambda tt, j=j, oc=oc, bank=bank,
                                        base=base: tt.matmul(
                                            bank[:, 0:TQ],
                                            r4t[:, oc, j, :],
                                            r3[:, j, base:base + TQ],
                                            start=(j == 0), stop=(j == CCH - 1)),
                                        reads=[f"R4_{oc}", f"R3_{j}"],
                                        writes=([bky] if j in (0, CCH - 1) else []))
                        sg = sgbuf[:, (mi % 2) * 512:(mi % 2) * 512 + TQ]
                        sgk = [f"kvsb{mi % 2}"]
                        tb.emit(A, lambda a, sg=sg, gb=gb: a.activation(
                            sg[:, 0:TQ], gb[:, 0:TQ], AF.Silu),
                            reads=[gk], writes=sgk)
                        tb.emit(V, lambda v, sg=sg, ub=ub, hrow=hrow:
                                v.tensor_tensor(hrow[:, 0:TQ], sg[:, 0:TQ],
                                                ub[:, 0:TQ], op=ALU.mult),
                                reads=sgk + [uk_], writes=hk)
                    for ti in range(nq):
                        t = qr * nq + ti
                        for piece in range(2):
                            db = ps45[:, 0:512] if piece == 0 else ps45[:, 512:1024]
                            dbk = "pb4" if piece == 0 else "pb5"
                            for m in range(ICH):
                                hrow = r6v(m * TQ * 2, BF16, (TQ,))
                                hk = r6k(m * TQ * 2, TQ * 2)
                                tb.emit(T, lambda tt, m=m, ti=ti, piece=piece,
                                        db=db, hrow=hrow: tt.matmul(
                                            db, hrow[:, ti * P:(ti + 1) * P],
                                            r5[:, m, piece * 512:(piece + 1) * 512],
                                            start=(m == 0), stop=(m == ICH - 1)),
                                        reads=hk + [f"R5_{m}"],
                                        writes=([dbk] if m in (0, ICH - 1) else []))
                            tb.emit(V, lambda v, t=t, piece=piece, db=db, ee=ee:
                                    v.scalar_tensor_tensor(
                                        xv(t)[:, piece * 512:(piece + 1) * 512],
                                        db, w10[:, t, ee:ee + 1],
                                        xv(t)[:, piece * 512:(piece + 1) * 512],
                                        ALU.mult, ALU.add),
                                    reads=[dbk, "w10", xk(t)], writes=[xk(t)])
                if ee + 1 < NE:
                    bf16_load(ee + 1)
                if ee + 2 < NE:
                    prepass_expert(ee + 2)

            for t in range(tch):
                tb.dma(S, "dy", lambda s, t=t: s.dma_start(
                    out=y[t * P:(t + 1) * P, :], in_=xv(t)),
                    reads=[xk(t)])
            tb.eng["sp"].wait_ge(tb.sem["dy"], tb.dcnt["dy"])

    return nc


def _prep_inputs(L0_attn_x, L0_attn_w_qkv, L0_attn_w_proj, L0_attn_b_proj,
                 L0_attn_w_dwc, L0_attn_b_dwc, L0_ffn_gate_w,
                 L0_ffn_experts_gate_up, L0_ffn_experts_down,
                 L0_ffn_shared_gate_w, L0_ffn_shared_up_w, L0_ffn_shared_down_w,
                 L0_ffn_shared_expert_gate_w):
    f32 = np.float32
    x = np.ascontiguousarray(np.asarray(L0_attn_x, dtype=f32).reshape(-1, C))
    ntok = x.shape[0]
    per = ntok // NCORES
    def piece_tile(arrT):
        Cc, ncols = arrT.shape
        nk = ncols // P
        out = np.empty((nk, P, Cc), f32)
        for k in range(nk):
            out[k] = arrT[:, k * P:(k + 1) * P].reshape(
                Cc // P, P, P).transpose(1, 0, 2).reshape(P, Cc)
        return out

    wqkvT_full = np.asarray(L0_attn_w_qkv, f32).T
    wqkvP_ = piece_tile(np.ascontiguousarray(wqkvT_full))
    wprojT_ = np.ascontiguousarray(np.asarray(L0_attn_w_proj, f32).T)
    gate9 = np.concatenate([np.asarray(L0_ffn_gate_w, f32),
                            np.asarray(L0_ffn_shared_expert_gate_w, f32)], axis=0)
    gate9P_ = np.ascontiguousarray(
        gate9.T.reshape(CCH, P, 9).transpose(1, 0, 2))
    bprojr_ = np.ascontiguousarray(np.asarray(L0_attn_b_proj, f32).reshape(1, C))
    order = []
    for m in range(ICH):
        order += [m, ICH + m]
    egu = np.asarray(L0_ffn_experts_gate_up, f32)
    sg = np.asarray(L0_ffn_shared_gate_w, f32)
    su = np.asarray(L0_ffn_shared_up_w, f32)
    sd = np.asarray(L0_ffn_shared_down_w, f32)
    edn = np.asarray(L0_ffn_experts_down, f32)
    wguP_ = np.empty((NE, 2 * ICH, P, C), f32)
    wdnT_ = np.empty((NE, 1408, C), f32)
    for ee in range(8):
        rows = np.concatenate([egu[ee][o * P:(o + 1) * P] for o in order], axis=0)
        wguP_[ee] = piece_tile(np.ascontiguousarray(rows.T))
        wdnT_[ee] = edn[ee].T
    for pi in range(2):
        r0, r1 = pi * 1408, (pi + 1) * 1408
        stacked = np.concatenate([sg[r0:r1], su[r0:r1]], axis=0)
        rows = np.concatenate([stacked[o * P:(o + 1) * P] for o in order], axis=0)
        wguP_[8 + pi] = piece_tile(np.ascontiguousarray(rows.T))
        wdnT_[8 + pi] = np.ascontiguousarray(sd[:, r0:r1].T)
    wdc = np.asarray(L0_attn_w_dwc, f32)
    dwcw_ = np.ascontiguousarray(wdc.reshape(CCH, P, 3).transpose(1, 0, 2))
    dwcb_ = np.ascontiguousarray(np.asarray(L0_attn_b_dwc, f32).reshape(CCH, P).T)
    identw_ = np.eye(P, dtype=ml_dtypes.bfloat16)

    in_maps = []
    for i in range(NCORES):
        s = i * per
        hl = x[s - 1] if i > 0 else np.zeros(C, f32)
        hr = x[s + per] if i < NCORES - 1 else np.zeros(C, f32)
        in_maps.append({
            "xin": np.ascontiguousarray(x[s:s + per]),
            "xhalo": np.ascontiguousarray(np.stack([hl, hr])),
            "hmask": np.array([[1.0 if i > 0 else 0.0],
                               [1.0 if i < NCORES - 1 else 0.0]], f32),
            "wqkvP": wqkvP_, "wprojT": wprojT_, "gate9P": gate9P_,
            "bprojr": bprojr_, "wguP": wguP_, "wdnT": wdnT_,
            "dwcw": dwcw_, "dwcb": dwcb_, "identw": identw_,
        })
    return in_maps, x.shape


def kernel(**inputs):
    in_maps, (ntok, _) = _prep_inputs(**inputs)
    tch = ntok // NCORES // P
    nc = build_nc(tch=tch)
    res = run_bass_kernel_spmd(nc, in_maps, core_ids=list(range(NCORES)))
    out = np.concatenate([r["y"] for r in res.results], axis=0)
    return out.reshape(np.asarray(inputs["L0_attn_x"]).shape)


# version 13 (final)
# speedup vs baseline: 1.5597x; 1.4673x over previous
import numpy as np
import ml_dtypes
import concourse.bass as bass
import concourse.mybir as mybir
from concourse.bass_utils import run_bass_kernel_spmd
from contextlib import ExitStack

F32 = mybir.dt.float32
BF16 = mybir.dt.bfloat16
AF = mybir.ActivationFunctionType
ALU = mybir.AluOpType
AX = mybir.AxisListType

NCORES = 8
P = 128
C = 1024
CCH = 8
H = 16
D = 64
OQKV = 3072
ICH = 11
NE = 10
EPS_A = 1e-6
EPS_LN = 1e-5
GU_GROUPS = [(0, 6), (6, 12), (12, 18), (18, 22)]
WDN_HALVES = [(0, 6), (6, 11)]


class TB:
    COMPUTE = ("pe", "act", "dve", "pool", "sp")

    def __init__(self, nc, ctx):
        self.nc = nc
        self.eng = {"pe": nc.tensor, "act": nc.scalar, "dve": nc.vector,
                    "pool": nc.gpsimd, "sp": nc.sync}
        self.sem = {}
        self.cnt = {}
        for name in self.COMPUTE:
            self.sem[name] = ctx.enter_context(nc.semaphore(f"S_{name}"))
            self.cnt[name] = 0
        self.dcnt = {}
        self.last_w = {}
        self.readers = {}
        self.max_waited = {name: {} for name in self.COMPUTE}
        self.pos = {name: 0 for name in self.COMPUTE}
        self.mat_pos = {name: [] for name in self.COMPUTE}
        self.mat_ent = {name: [] for name in self.COMPUTE}
        self.dma_ents = {}

    def dma_sem(self, name, ctx):
        assert name not in self.sem
        self.sem[name] = ctx.enter_context(self.nc.semaphore(name))
        self.dcnt[name] = 0
        self.dma_ents[name] = []

    def _mat(self, ent):
        if ent["val"] is not None:
            return ent["sem"], ent["val"]
        en = ent["eng"]
        mp = self.mat_pos[en]
        p = ent["pos"]
        if mp and p < mp[-1]:
            import bisect
            i = bisect.bisect_left(mp, p)
            sub = self.mat_ent[en][i]
            return sub["sem"], sub["val"]
        self.cnt[en] += 1
        ent["val"] = self.cnt[en]
        ent["sem"] = en
        ent["inst"].then_inc(self.sem[en], 1)
        mp.append(p)
        self.mat_ent[en].append(ent)
        return en, ent["val"]

    def _deps(self, eng_name, reads, writes):
        deps = {}

        def add(ent):
            s, v = self._mat(ent)
            if deps.get(s, 0) < v:
                deps[s] = v

        for k in reads:
            ent = self.last_w.get(k)
            if ent is not None:
                add(ent)
        for k in writes:
            ent = self.last_w.get(k)
            if ent is not None:
                add(ent)
            for r in self.readers.get(k, ()):
                add(r)
        mw = self.max_waited[eng_name]
        return [(s, v) for s, v in deps.items() if mw.get(s, 0) < v]

    def _register(self, ent, reads, writes):
        for k in writes:
            self.last_w[k] = ent
            self.readers[k] = []
        for k in reads:
            self.readers.setdefault(k, []).append(ent)

    def _emit(self, eng_name, fn, reads, writes, dma_sem=None):
        deps = self._deps(eng_name, reads, writes)
        engobj = self.eng[eng_name]
        mw = self.max_waited[eng_name]
        for s, v in deps[:-1]:
            engobj.wait_ge(self.sem[s], v)
            mw[s] = max(mw.get(s, 0), v)
        inst = fn(engobj)
        if deps:
            s, v = deps[-1]
            inst._wait_ge(self.sem[s], v)
            mw[s] = max(mw.get(s, 0), v)
        if dma_sem is None:
            self.pos[eng_name] += 1
            ent = {"eng": eng_name, "inst": inst, "sem": None, "val": None,
                   "pos": self.pos[eng_name]}
        else:
            self.dcnt[dma_sem] += 16
            inst.then_inc(self.sem[dma_sem], 16)
            ent = {"eng": eng_name, "inst": inst, "sem": dma_sem,
                   "val": self.dcnt[dma_sem]}
            self.dma_ents[dma_sem].append(ent)
        self._register(ent, reads, writes)
        return ent

    def emit(self, eng_name, fn, reads=(), writes=()):
        return self._emit(eng_name, fn, list(reads), list(writes))

    def dma(self, q, semname, fn, reads=(), writes=()):
        return self._emit(q, fn, list(reads), list(writes), dma_sem=semname)

    def alias_key(self, new, old):
        if old in self.last_w:
            self.last_w[new] = self.last_w[old]
        self.readers[new] = list(self.readers.get(old, ()))

    def seal_dma_group(self, semname):
        total = self.dcnt[semname]
        for ent in self.dma_ents[semname]:
            ent["val"] = total


def _span_keys(prefix, off, nbytes, blk=2048):
    return [f"{prefix}{b}" for b in range(off // blk, (off + nbytes - 1) // blk + 1)]


def build_nc(tch=16, dbg=False):
    union = (tch == 16)
    TOK = tch * P
    TW = TOK + 2
    TH = TOK // 2
    nhalf = tch // 2
    QP = [(0, "q"), (1, "q"), (2, "k"), (3, "k"), (4, "v"), (5, "v")]
    R6B = 22528
    O_LN1 = [0, 2048]
    O_QC = [4096, 6144]
    O_KC3 = 8192
    O_KN = 10240
    O_S1 = [12288, 14336]
    O_S2 = [16384, 17408]
    O_VE = 18432
    O_CU = 0
    O_AOT = [4 * TOK, 4 * TOK + 2048]
    O_LN2 = [4 * TOK + 4096, 4 * TOK + 6144]
    O_RT = 4 * TOK + 8192
    assert O_RT + 64 * tch + 2048 <= R6B + 2048

    nc = bass.Bass("TRN2", target_bir_lowering=False)

    xin = nc.dram_tensor("xin", [TOK, C], F32, kind="ExternalInput")
    xhalo = nc.dram_tensor("xhalo", [2, C], F32, kind="ExternalInput")
    hmask = nc.dram_tensor("hmask", [2, 1], F32, kind="ExternalInput")
    wqkvT = nc.dram_tensor("wqkvT", [C, OQKV], F32, kind="ExternalInput")
    wprojT = nc.dram_tensor("wprojT", [C, C], F32, kind="ExternalInput")
    gate9T = nc.dram_tensor("gate9T", [C, 9], F32, kind="ExternalInput")
    bprojr = nc.dram_tensor("bprojr", [1, C], F32, kind="ExternalInput")
    wguT = nc.dram_tensor("wguT", [NE, C, 2816], F32, kind="ExternalInput")
    wdnT = nc.dram_tensor("wdnT", [NE, 1408, C], F32, kind="ExternalInput")
    dwcw = nc.dram_tensor("dwcw", [P, CCH, 3], F32, kind="ExternalInput")
    dwcb = nc.dram_tensor("dwcb", [P, CCH], F32, kind="ExternalInput")
    identw = nc.dram_tensor("identw", [P, P], BF16, kind="ExternalInput")
    y = nc.dram_tensor("y", [TOK, C], F32, kind="ExternalOutput")
    dbgx = nc.dram_tensor("dbgx", [TOK, C], F32, kind="ExternalOutput") if dbg else None
    dbgkv = nc.dram_tensor("dbgkv", [D, H * 65], F32, kind="ExternalOutput") if dbg else None
    dbgkn = nc.dram_tensor("dbgkn", [P, C], BF16, kind="ExternalOutput") if dbg else None
    dbgve = nc.dram_tensor("dbgve", [P, H * 65], BF16, kind="ExternalOutput") if dbg else None
    dbgqc = nc.dram_tensor("dbgqc", [P, C], BF16, kind="ExternalOutput") if dbg else None
    cc_in = nc.dram_tensor("cc_in", [D, H * 65], F32)
    cc_out = nc.dram_tensor("cc_out", [D, H * 65], F32)

    with ExitStack() as ctx:
        e = ctx.enter_context
        tb = TB(nc, ctx)
        for s in ("dx0", "dx1", "dx2", "dx3", "dxr", "dwq", "dsm", "dsw",
                  "dgu0", "dgu1", "dgu2", "dgu3", "ddn0", "ddn1", "dcc", "dy", "ddbg", "dwq1", "dwq2"):
            tb.dma_sem(s, ctx)

        rx = e(nc.sbuf_tensor("rx", [P, tch * C], F32))
        if union:
            wqkv_bf = rx[:, 0:12 * C].bitcast(BF16).rearrange(
                "p (j o) -> p j o", o=OQKV)
            wq_keys = lambda j: _span_keys("Rx", j * OQKV * 2, OQKV * 2, 4096)
        else:
            wqkv_t = e(nc.sbuf_tensor("wqkv_t", [P, CCH, OQKV], BF16))
            wqkv_bf = wqkv_t[:, :, :]
            wq_keys = lambda j: [f"wq{j}"]

        xv = lambda t: rx[:, t * C:(t + 1) * C]
        ring_slot = lambda t: (12 + t % 4) if union else t
        xk = lambda t: f"Rx{t}"

        r3 = e(nc.sbuf_tensor("r3", [P, CCH, TW], BF16))
        r4 = e(nc.sbuf_tensor("r4", [P, CCH, 2816], BF16))
        r5 = e(nc.sbuf_tensor("r5", [P, ICH, C], BF16))
        r6 = e(nc.sbuf_tensor("r6", [P, R6B // 2], BF16))

        def r6v(off, dtype, shape):
            n = int(np.prod(shape))
            nbytes = n * mybir.dt.size(dtype)
            ap = r6[:, off // 2: (off + nbytes) // 2]
            if dtype != BF16:
                ap = ap.bitcast(dtype)
            if len(shape) > 1:
                ap = ap.rearrange("p (a b) -> p a b", b=shape[-1])
            return ap

        r6k = lambda off, nbytes: _span_keys("R6", off, nbytes)

        GB = [(a * P, b * P) for a, b in GU_GROUPS]

        def r4k(j, c0, c1):
            return [f"R4_{j}_{gi}" for gi, (a, b) in enumerate(GB)
                    if not (c1 <= a or c0 >= b)]

        ident = e(nc.sbuf_tensor("ident", [P, P], BF16))
        bnst = e(nc.sbuf_tensor("bnst", [P, 2, 12], F32))
        mv = e(nc.sbuf_tensor("mv", [P, 2], F32))
        mstat = e(nc.sbuf_tensor("mstat", [P, tch + 1, 2], F32))
        sk2 = e(nc.sbuf_tensor("sk2", [P, H], F32))
        kv_sb = e(nc.sbuf_tensor("kv_sb", [P, H * 65], F32))
        kvg_bf = e(nc.sbuf_tensor("kvg_bf", [P, H, 65], BF16))
        g_all = e(nc.sbuf_tensor("g_all", [P, 2, H], F32))
        w10 = e(nc.sbuf_tensor("w10", [P, tch, 10], F32))
        bproj_bf = e(nc.sbuf_tensor("bproj_bf", [1, C], BF16))
        ones_c = e(nc.sbuf_tensor("ones_c", [1, P], BF16))
        gate9_bf = e(nc.sbuf_tensor("gate9_bf", [P, CCH, 9], BF16))
        dwcw_sb = e(nc.sbuf_tensor("dwcw_sb", [P, CCH, 3], F32))
        dwcb_sb = e(nc.sbuf_tensor("dwcb_sb", [P, CCH], F32))
        hm_sb = e(nc.sbuf_tensor("hm_sb", [2, 1], F32))
        cst_eps = e(nc.sbuf_tensor("cst_eps", [P, 1], F32))
        xh_sb = e(nc.sbuf_tensor("xh_sb", [2, C], F32))
        ln1h_tok = e(nc.sbuf_tensor("ln1h_tok", [2, C], BF16))
        ln1h_fm = e(nc.sbuf_tensor("ln1h_fm", [P, CCH, 2], BF16))
        qchT = e(nc.sbuf_tensor("qchT", [P, CCH, 2], BF16))
        gh = e(nc.sbuf_tensor("gh", [2, H], F32))
        hstat = e(nc.sbuf_tensor("hstat", [2, 16], F32))
        qch_tok = xh_sb[0:2, 0:512].bitcast(BF16)
        aoh_tok = ln1h_tok
        lg_all = r6v(O_RT, F32, (tch, 9))
        ex_all = r6v(O_RT + tch * 36, F32, (tch, 8))
        sc_all = r6v(O_RT + tch * 68, F32, (tch, 8))
        mx8 = r6v(O_RT + tch * 100, F32, (tch, 8))
        rtmp = r6v(O_RT + tch * 132, F32, (tch, 2))
        RTN = tch * 140
        k_lg = r6k(O_RT, tch * 36)
        k_ex = r6k(O_RT + tch * 36, tch * 32)
        k_sc = r6k(O_RT + tch * 68, tch * 32)
        k_mx = r6k(O_RT + tch * 100, tch * 32)
        k_rt = r6k(O_RT + tch * 132, tch * 8)

        ps0 = e(nc.psum_tensor("ps0", [P, 512], F32))
        ps1 = e(nc.psum_tensor("ps1", [P, 512], F32))
        ps2 = e(nc.psum_tensor("ps2", [P, 512], F32))
        ps3 = e(nc.psum_tensor("ps3", [P, 512], F32))
        ps45 = e(nc.psum_tensor("ps45", [P, 1024], F32))
        ps6 = e(nc.psum_tensor("ps6", [P, 512], F32))
        ps7 = e(nc.psum_tensor("ps7", [P, 512], F32))
        kvbank = [(ps0, "pb0", 0, 7), (ps1, "pb1", 7, 14), (ps6, "pb6", 14, 16)]

        block = e(nc.Block())

        @block.vector
        def _build(_v):
            V, A, T, G, S = "dve", "act", "pe", "pool", "sp"

            tb.dma(S, "dsm", lambda s: s.dma_start(out=ident[:, :], in_=identw[:, :]),
                   writes=["ident"])
            tb.dma(S, "dsm", lambda s: s.dma_start(out=xh_sb[:, :], in_=xhalo[:, :]),
                   writes=["xh"])
            tb.dma(S, "dsm", lambda s: s.dma_start(out=hm_sb[:, :], in_=hmask[:, :]),
                   writes=["hm"])
            tb.dma(S, "dsm", lambda s: s.dma_start(out=dwcw_sb[:, :, :], in_=dwcw[:, :, :]),
                   writes=["dwcw"])
            tb.dma(S, "dsm", lambda s: s.dma_start(out=dwcb_sb[:, :], in_=dwcb[:, :]),
                   writes=["dwcb"])
            for wi, sem in enumerate(("dwq", "dwq1", "dwq2")):
                c0, c1 = wi * 1024, (wi + 1) * 1024
                tb.dma(G, sem, lambda g, c0=c0, c1=c1: g.dma_start(
                    out=wqkv_bf[:, :, c0:c1],
                    in_=wqkvT[:, c0:c1].rearrange("(j p) o -> p j o", p=P)),
                    writes=[k for j in range(CCH)
                            for k in (_span_keys("Rx", j * OQKV * 2 + c0 * 2,
                                                 (c1 - c0) * 2, 4096)
                                      if union else [f"wq{j}"])])
            tb.dma(G, "dsw", lambda g: g.dma_start(
                out=r5[:, 0:CCH, :], in_=wprojT[:, :].rearrange("(j p) o -> p j o", p=P)),
                writes=[f"R5_{m}" for m in range(CCH)])
            tb.dma(G, "dsw", lambda g: g.dma_start(
                out=gate9_bf[:, :, :], in_=gate9T[:, :].rearrange("(j p) o -> p j o", p=P)),
                writes=["gate9"])
            tb.dma(G, "dsw", lambda g: g.dma_start(out=bproj_bf[:, :], in_=bprojr[:, :]),
                   writes=["bproj"])
            tb.seal_dma_group("dsm")
            tb.seal_dma_group("dsw")

            def ring_load(t):
                sl = ring_slot(t)
                tb.dma(S, f"dx{t % 4}",
                       lambda s, t=t, sl=sl: s.dma_start(
                           out=rx[:, sl * C:(sl + 1) * C],
                           in_=xin[t * P:(t + 1) * P, :]),
                       writes=[f"Rx{sl}"])
            for t in range(min(4, tch)):
                ring_load(t)
            tb.emit(G, lambda g: g.memset(ones_c[:, :], 1.0), writes=["ones"])
            tb.emit(V, lambda v: v.memset(cst_eps[:, :], EPS_LN), writes=["csteps"])
            tb.emit(V, lambda v: v.memset(kv_sb[0:D, :], 0.0), writes=["kvsb"])
            tb.emit(V, lambda v: v.memset(
                r6v(O_VE, BF16, (H, 65))[:, :, 64:65], 1.0),
                writes=r6k(O_VE, 2080))
            tb.emit(V, lambda v: v.tensor_scalar(
                dwcw_sb[:, :, 1:2], dwcw_sb[:, :, 1:2], 1.0, None, ALU.add),
                reads=["dwcw"], writes=["dwcw"])

            def emit_ln(x_ap, x_keys, pp, stat_idx, out_ap, out_keys, par):
                if pp == P:
                    bn, bnk = bnst[:, par, :], f"bnst{par}"
                    ag, agk = mv[:, :], "mv"
                    ms, msk = mstat[:, stat_idx, :], f"mstat{stat_idx}"
                else:
                    bn, bnk = hstat[0:2, 0:12], "hstat"
                    ag, agk = hstat[0:2, 12:14], "hstat1"
                    ms, msk = hstat[0:2, 14:16], "hstat2"
                tb.emit(V, lambda v: v.bn_stats(bn[:, 0:6], x_ap[:, 0:512]),
                        reads=x_keys, writes=[bnk])
                tb.emit(V, lambda v: v.bn_stats(bn[:, 6:12], x_ap[:, 512:1024]),
                        reads=[bnk] + x_keys, writes=[bnk])
                tb.emit(V, lambda v: v.bn_aggr(ag, bn), reads=[bnk], writes=[agk])
                tb.emit(V, lambda v: v.tensor_copy(ms[:, 0:1], ag[:, 0:1]),
                        reads=[agk], writes=[msk])
                tb.emit(A, lambda a: a.activation(ms[:, 1:2], ag[:, 1:2], AF.Sqrt,
                                                  bias=cst_eps[0:pp, 0:1]),
                        reads=[agk, "csteps"], writes=[msk])
                tb.emit(V, lambda v: v.reciprocal(ms[:, 1:2], ms[:, 1:2]),
                        reads=[msk], writes=[msk])
                tb.emit(V, lambda v: v.tensor_scalar(
                    out_ap, x_ap, ms[:, 0:1], ms[:, 1:2], ALU.subtract, ALU.mult),
                    reads=x_keys + [msk], writes=out_keys)

            def emit_tr8(src_ap, src_keys, dst_ap, dst_keys, pp):
                for j in range(CCH):
                    tb.emit(T, lambda t, j=j: t.matmul(
                        ps45[:, j * pp:(j + 1) * pp],
                        src_ap[:, j * P:(j + 1) * P], ident[0:pp, 0:pp],
                        start=True, stop=True),
                        reads=list(src_keys) + ["ident"],
                        writes=(["pb4", "pb5"] if j in (0, CCH - 1) else []))
                tb.emit(A, lambda a: a.activation(
                    dst_ap, ps45[:, 0:CCH * pp].rearrange("p (j q) -> p j q", q=pp),
                    AF.Copy),
                    reads=["pb4", "pb5"], writes=list(dst_keys))

            def phase_a_chunk(t, halo):
                par = t % 2
                if halo:
                    pp = 2
                    xap, xkeys = xh_sb[0:2, :], ["xh"]
                    ln1, ln1k = ln1h_tok[0:2, :], ["ln1h"]
                    pieces = QP[0:2]
                else:
                    pp = P
                    sl = ring_slot(t)
                    xap, xkeys = rx[:, sl * C:(sl + 1) * C], [f"Rx{sl}"]
                    ln1 = r6v(O_LN1[par], BF16, (C,))
                    ln1k = r6k(O_LN1[par], 2048)
                    pieces = QP
                emit_ln(xap, xkeys, pp, tch if halo else t, ln1, ln1k, par)
                if halo:
                    emit_tr8(ln1, ln1k, ln1h_fm[:, :, :], ["ln1hfm"], 2)
                else:
                    col = 1 + t * P
                    emit_tr8(ln1, ln1k, r3[:, :, col:col + P],
                             [f"R3_{j}" for j in range(CCH)], P)
                if halo:
                    qc, qck = qch_tok, ["xh"]
                else:
                    qc = r6v(O_QC[par], BF16, (C,))
                    qck = r6k(O_QC[par], 2048)
                kc3 = r6v(O_KC3, BF16, (C,))
                kc3k = r6k(O_KC3, 2048)
                kn = r6v(O_KN, BF16, (C,))
                knk = r6k(O_KN, 2048)
                ve = r6v(O_VE, BF16, (H, 65))
                vek = r6k(O_VE, 2080)

                for pc, kind in pieces:
                    if halo:
                        pbank, pbk = ps7, "pb7"
                    else:
                        pbank, pbk = (ps2, "pb2") if pc % 2 == 0 else (ps3, "pb3")
                    out_ps = pbank[0:pp, :]
                    for j in range(CCH):
                        if halo:
                            lhsT, lk = ln1h_fm[:, j, 0:2], ["ln1hfm"]
                        else:
                            lhsT, lk = r3[:, j, 1 + t * P:1 + (t + 1) * P], [f"R3_{j}"]
                        tb.emit(T, lambda tt, j=j, pc=pc, lhsT=lhsT, out_ps=out_ps:
                                tt.matmul(out_ps, lhsT,
                                          wqkv_bf[:, j, pc * 512:(pc + 1) * 512],
                                          start=(j == 0), stop=(j == CCH - 1)),
                                reads=lk + (_span_keys(
                                    "Rx", j * OQKV * 2 + pc * 1024, 1024, 4096)
                                    if union else [f"wq{j}"]),
                                writes=([pbk] if j in (0, CCH - 1) else []))
                    spar = pc % 2
                    s1 = r6v(O_S1[spar], F32, (512,))[0:pp, :]
                    s1k = r6k(O_S1[spar], 2048)
                    s2 = r6v(O_S2[spar], BF16, (512,))[0:pp, :]
                    s2k = r6k(O_S2[spar], 1024)
                    off = (pc % 2) * 512
                    if kind == "q":
                        dst, dstk = qc[0:pp, off:off + 512], qck
                    elif kind == "k":
                        dst, dstk = kc3[0:pp, off:off + 512], kc3k
                    if kind in ("q", "k"):
                        tb.emit(V, lambda v, s1=s1, out_ps=out_ps: v.tensor_scalar(
                            s1, out_ps, 0.0, EPS_A, ALU.max, ALU.add),
                            reads=[pbk], writes=s1k)
                        tb.emit(A, lambda a, s1=s1, s2=s2: a.activation(
                            s2, s1, AF.Square), reads=s1k, writes=s2k)
                        tb.emit(G, lambda g, s1=s1, s2=s2, dst=dst: g.tensor_tensor(
                            dst, s2, s1, op=ALU.mult),
                            reads=s1k + s2k, writes=dstk)
                    else:
                        tb.emit(A, lambda a, out_ps=out_ps, pc=pc: a.activation(
                            ve[:, (pc - 4) * 8:(pc - 4) * 8 + 8, 0:64],
                            out_ps.rearrange("p (h d) -> p h d", d=64), AF.Copy),
                            reads=[pbk], writes=vek)
                if halo:
                    return
                tb.emit(V, lambda v: v.tensor_reduce(
                    sk2[:, :].unsqueeze(2),
                    kc3.rearrange("p (h d) -> p h d", d=64),
                    axis=AX.X, op=ALU.add),
                    reads=kc3k, writes=["sk"])
                tb.emit(V, lambda v: v.reciprocal(sk2[:, :], sk2[:, :]),
                        reads=["sk"], writes=["sk"])
                tb.emit(G, lambda g: g.tensor_tensor(
                    kn.rearrange("p (h d) -> p h d", d=64),
                    kc3.rearrange("p (h d) -> p h d", d=64),
                    sk2[:, :].unsqueeze(2).to_broadcast([P, H, 64]), op=ALU.mult),
                    reads=kc3k + ["sk"], writes=knk)
                for h in range(H):
                    bank, bkey, lo, hi = kvbank[h // 7 if h < 14 else 2]
                    off = (h - lo) * 65
                    tb.emit(T, lambda tt, h=h, bank=bank, off=off: tt.matmul(
                                bank[0:D, off:off + 65],
                                kn[:, h * D:(h + 1) * D], ve[:, h, :],
                                start=True, stop=True),
                            reads=knk + vek, writes=[bkey])
                for bank, bkey, lo, hi in kvbank:
                    tb.emit(V, lambda v, bank=bank, lo=lo, hi=hi: v.tensor_tensor(
                        kv_sb[0:D, lo * 65:hi * 65], kv_sb[0:D, lo * 65:hi * 65],
                        bank[0:D, 0:(hi - lo) * 65], op=ALU.add),
                        reads=[bkey, "kvsb"], writes=["kvsb"])
                if dbg and t == 5:
                    tb.dma(S, "ddbg", lambda s: s.dma_start(out=dbgkn[:, :], in_=kn),
                           reads=knk)
                    tb.dma(S, "ddbg", lambda s: s.dma_start(
                        out=dbgve[:, :], in_=ve.rearrange("p a b -> p (a b)")),
                        reads=vek)
                    tb.dma(S, "ddbg", lambda s: s.dma_start(out=dbgqc[:, :], in_=qc),
                           reads=qck)
                    tb.seal_dma_group("ddbg")
                col = 1 + t * P
                emit_tr8(qc, qck, r4[:, :, col:col + P],
                         [k for j in range(CCH) for k in r4k(j, col, col + P)], P)

            for t in range(tch):
                phase_a_chunk(t, halo=False)
                if t + 4 < tch:
                    ring_load(t + 4)
            phase_a_chunk(0, halo=True)
            emit_tr8(qch_tok, ["xh"], qchT[:, :, :], ["qchT"], 2)

            tb.dma(G, "dcc", lambda g: g.dma_start(out=cc_in[:, :], in_=kv_sb[0:D, :]),
                   reads=["kvsb"], writes=["ccin"])
            tb.emit(G, lambda g: g.collective_compute(
                "AllReduce", ALU.add, replica_groups=[list(range(NCORES))],
                ins=[cc_in[:, :].opt()], outs=[cc_out[:, :].opt()]),
                reads=["ccin"], writes=["ccout"])
            tb.dma(G, "dcc", lambda g: g.dma_start(out=kv_sb[0:D, :], in_=cc_out[:, :]),
                   reads=["ccout"], writes=["kvsb"])
            tb.dma(G, "dcc", lambda g: g.dma_start(out=kv_sb[D:P, :], in_=cc_out[:, :]),
                   reads=["ccout"], writes=["kvsb"])
            kvv = kv_sb[:, :].rearrange("p (h e) -> p h e", e=65)
            tb.emit(V, lambda v: v.tensor_scalar(
                kvv[:, :, 64:65], kvv[:, :, 64:65], EPS_A, None, ALU.add),
                reads=["kvsb"], writes=["kvsb"])
            tb.emit(V, lambda v: v.tensor_copy(kvg_bf[:, :, :], kvv),
                    reads=["kvsb"], writes=["kvg"])
            if dbg:
                tb.dma(S, "ddbg", lambda s: s.dma_start(
                    out=dbgkv[:, :], in_=kv_sb[0:D, :]), reads=["kvsb"])

            if union:
                for t in range(12):
                    tb.dma(S, "dxr", lambda s, t=t: s.dma_start(
                        out=xv(t), in_=xin[t * P:(t + 1) * P, :]),
                        writes=[xk(t)])
                tb.seal_dma_group("dxr")

            def emit_attn(t, halo):
                pp = 2 if halo else P
                par = 0 if halo else t % 2
                if halo:
                    ao, aok = aoh_tok[0:2, :], ["ln1h"]
                    gg, ggk = gh[0:2, :], "gh"
                else:
                    ao = r6v(O_AOT[par], BF16, (C,))
                    aok = r6k(O_AOT[par], 2048)
                    gg, ggk = g_all[:, par, :], f"gall{par}"
                absets = [kvbank,
                          [(ps2, "pb2", 0, 7), (ps3, "pb3", 7, 14),
                           (ps7, "pb7", 14, 16)]]
                for bank, bkey, lo, hi in absets[0 if halo else t % 2]:
                    nh = hi - lo
                    for h in range(lo, hi):
                        if halo:
                            lhsT = qchT[(h % 2) * D:(h % 2) * D + D, h // 2, 0:2]
                            lk = ["qchT"]
                        else:
                            cc0 = 1 + t * P
                            lhsT = r4[(h % 2) * D:(h % 2) * D + D, h // 2,
                                      cc0:cc0 + P]
                            lk = r4k(h // 2, cc0, cc0 + P)
                        tb.emit(T, lambda tt, bank=bank, h=h, lo=lo, hi=hi,
                                lhsT=lhsT: tt.matmul(
                                    bank[0:pp, (h - lo) * 65:(h - lo) * 65 + 65],
                                    lhsT, kvg_bf[(h % 2) * D:(h % 2) * D + D, h, :],
                                    start=True, stop=True),
                                reads=lk + ["kvg"], writes=[bkey])
                    bview = bank[0:pp, 0:nh * 65].rearrange("p (h e) -> p h e", e=65)
                    tb.emit(V, lambda v, bview=bview, lo=lo, hi=hi: v.reciprocal(
                        gg[0:pp, lo:hi], bview[:, :, 64:65].squeeze(2)),
                        reads=[bkey], writes=[ggk])
                    tb.emit(V, lambda v, bview=bview, lo=lo, hi=hi, nh=nh:
                            v.tensor_tensor(
                                ao[0:pp, lo * D:hi * D].rearrange(
                                    "p (h d) -> p h d", d=D),
                                bview[:, :, 0:64],
                                gg[0:pp, lo:hi].unsqueeze(2).to_broadcast(
                                    [pp, nh, D]),
                                op=ALU.mult),
                            reads=[bkey, ggk], writes=aok)
                if halo:
                    tb.emit(V, lambda v: v.tensor_scalar(
                        ao[0:2, :], ao[0:2, :], hm_sb[0:2, 0:1], None, ALU.mult),
                        reads=aok + ["hm"], writes=aok)
                    for j in range(CCH):
                        tb.emit(T, lambda tt, j=j: tt.matmul(
                            ps45[:, j * 2:(j + 1) * 2], ao[0:2, j * P:(j + 1) * P],
                            ident[0:2, 0:2], start=True, stop=True),
                            reads=aok + ["ident"],
                            writes=(["pb4", "pb5"] if j in (0, CCH - 1) else []))
                    pv = ps45[:, 0:2 * CCH].rearrange("p (j q) -> p j q", q=2)
                    tb.emit(A, lambda a: a.activation(
                        r3[:, :, 0:1], pv[:, :, 0:1], AF.Copy),
                        reads=["pb4", "pb5"],
                        writes=[f"R3_{j}" for j in range(CCH)])
                    tb.emit(A, lambda a: a.activation(
                        r3[:, :, TW - 1:TW], pv[:, :, 1:2], AF.Copy),
                        reads=["pb4", "pb5"],
                        writes=[f"R3_{j}" for j in range(CCH)])
                else:
                    col = 1 + t * P
                    emit_tr8(ao, aok, r3[:, :, col:col + P],
                             [f"R3_{j}" for j in range(CCH)], P)

            for t in range(tch):
                emit_attn(t, halo=False)
            emit_attn(0, halo=True)

            for j in range(CCH):
                u = r6v(O_CU, F32, (TOK,))
                uk = r6k(O_CU, 4 * TOK)
                tb.emit(V, lambda v, j=j, u=u: v.tensor_scalar(
                    u, r3[:, j, 2:2 + TOK], dwcw_sb[:, j, 2:3], dwcb_sb[:, j:j + 1],
                    ALU.mult, ALU.add),
                    reads=[f"R3_{j}", "dwcw", "dwcb"], writes=uk)
                tb.emit(V, lambda v, j=j, u=u: v.scalar_tensor_tensor(
                    u, r3[:, j, 0:TOK], dwcw_sb[:, j, 0:1], u, ALU.mult, ALU.add),
                    reads=[f"R3_{j}", "dwcw"] + uk, writes=uk)
                tb.emit(V, lambda v, j=j, u=u: v.scalar_tensor_tensor(
                    r4[:, j, 0:TOK], r3[:, j, 1:1 + TOK], dwcw_sb[:, j, 1:2], u,
                    ALU.mult, ALU.add),
                    reads=[f"R3_{j}", "dwcw"] + uk, writes=r4k(j, 0, TOK))

            def emit_proj(t):
                for piece in range(2):
                    if t % 2 == 0:
                        bank, bkey = (ps2, "pb2") if piece == 0 else (ps3, "pb3")
                    else:
                        bank, bkey = (ps0, "pb0") if piece == 0 else (ps1, "pb1")
                    for j in range(CCH):
                        tb.emit(T, lambda tt, j=j, piece=piece, t=t, bank=bank:
                                tt.matmul(bank[:, :],
                                          r4[:, j, t * P:(t + 1) * P],
                                          r5[:, j, piece * 512:(piece + 1) * 512],
                                          start=(j == 0), stop=False),
                                reads=r4k(j, t * P, (t + 1) * P) + [f"R5_{j}"],
                                writes=([bkey] if j == 0 else []))
                    tb.emit(T, lambda tt, piece=piece, bank=bank: tt.matmul(
                        bank[:, :], ones_c[0:1, :],
                        bproj_bf[0:1, piece * 512:(piece + 1) * 512],
                        start=False, stop=True),
                        reads=["ones", "bproj"], writes=[bkey])
                    tb.emit(V, lambda v, t=t, piece=piece, bank=bank: v.tensor_tensor(
                        xv(t)[:, piece * 512:(piece + 1) * 512],
                        xv(t)[:, piece * 512:(piece + 1) * 512],
                        bank[:, :], op=ALU.add),
                        reads=[xk(t), bkey], writes=[xk(t)])

            def emit_ln2chain(t):
                par = t % 2
                ln2 = r6v(O_LN2[par], BF16, (C,))
                ln2k = r6k(O_LN2[par], 2048)
                emit_ln(xv(t), [xk(t)], P, t, ln2, ln2k, par)

            def emit_trgate(t):
                par = t % 2
                ln2 = r6v(O_LN2[par], BF16, (C,))
                ln2k = r6k(O_LN2[par], 2048)
                col = t * P
                emit_tr8(ln2, ln2k, r3[:, :, col:col + P],
                         [f"R3_{j}" for j in range(CCH)], P)
                for j in range(CCH):
                    tb.emit(T, lambda tt, j=j, t=t: tt.matmul(
                        ps7[:, 0:9], r3[:, j, t * P:(t + 1) * P], gate9_bf[:, j, :],
                        start=(j == 0), stop=(j == CCH - 1)),
                        reads=[f"R3_{j}", "gate9"],
                        writes=(["pb7"] if j in (0, CCH - 1) else []))
                tb.emit(A, lambda a, t=t: a.activation(
                    lg_all[:, t, :], ps7[:, 0:9], AF.Copy),
                    reads=["pb7"], writes=k_lg)

            for t in range(tch):
                emit_proj(t)
                emit_ln2chain(t)
                if t > 0:
                    emit_trgate(t - 1)
            emit_trgate(tch - 1)

            if dbg:
                for t in range(tch):
                    tb.dma(S, "ddbg", lambda s, t=t: s.dma_start(
                        out=dbgx[t * P:(t + 1) * P, :], in_=xv(t)),
                        reads=[xk(t)])
                tb.seal_dma_group("ddbg")

            tb.emit(V, lambda v: v.tensor_reduce(
                rtmp[:, :, 0:1], lg_all[:, :, 0:8], axis=AX.X, op=ALU.max,
                negate=True),
                reads=k_lg, writes=k_rt)
            for t in range(tch):
                tb.emit(A, lambda a, t=t: a.activation(
                    ex_all[:, t, :], lg_all[:, t, 0:8], AF.Exp,
                    bias=rtmp[:, t, 0:1]),
                    reads=k_lg + k_rt, writes=k_ex)
            tb.emit(V, lambda v: v.tensor_reduce(
                rtmp[:, :, 1:2], ex_all[:, :, :], axis=AX.X, op=ALU.add),
                reads=k_ex, writes=k_rt)
            tb.emit(V, lambda v: v.reciprocal(rtmp[:, :, 1:2], rtmp[:, :, 1:2]),
                    reads=k_rt, writes=k_rt)
            tb.emit(V, lambda v: v.tensor_tensor(
                sc_all[:, :, :], ex_all[:, :, :],
                rtmp[:, :, 1:2].to_broadcast([P, tch, 8]), op=ALU.mult),
                reads=k_ex + k_rt, writes=k_sc)
            for t in range(tch):
                tb.emit(V, lambda v, t=t: v.max(out=mx8[:, t, :],
                                                in_=sc_all[:, t, :]),
                        reads=k_sc, writes=k_mx)
            tb.emit(V, lambda v: v.tensor_tensor(
                ex_all[:, :, :], sc_all[:, :, :],
                mx8[:, :, 1:2].to_broadcast([P, tch, 8]), op=ALU.is_ge),
                reads=k_sc + k_mx, writes=k_ex)
            tb.emit(V, lambda v: v.tensor_tensor(
                w10[:, :, 0:8], sc_all[:, :, :], ex_all[:, :, :], op=ALU.mult),
                reads=k_sc + k_ex, writes=["w10"])
            tb.emit(A, lambda a: a.activation(
                w10[:, :, 8:9], lg_all[:, :, 8:9], AF.Sigmoid),
                reads=k_lg, writes=["w10"])
            tb.emit(V, lambda v: v.tensor_copy(w10[:, :, 9:10], w10[:, :, 8:9]),
                    reads=["w10"], writes=["w10"])

            def guk(oc):
                for gi, (c0, c1) in enumerate(GU_GROUPS):
                    if c0 <= oc < c1:
                        return gi
                raise AssertionError

            def wgu_load(ee):
                for gi, (c0, c1) in enumerate(GU_GROUPS):
                    tb.dma(G, f"dgu{gi}", lambda g, ee=ee, c0=c0, c1=c1: g.dma_start(
                        out=r4[:, :, c0 * P:c1 * P],
                        in_=wguT[ee, :, c0 * P:c1 * P].rearrange(
                            "(j p) o -> p j o", p=P)),
                        writes=[f"R4_{j}_{gi}" for j in range(CCH)])

            def wdn_load(ee):
                for hi_, (m0, m1) in enumerate(WDN_HALVES):
                    tb.dma(G, f"ddn{hi_}", lambda g, ee=ee, m0=m0, m1=m1: g.dma_start(
                        out=r5[:, m0:m1, :],
                        in_=wdnT[ee, m0 * P:m1 * P, :].rearrange(
                            "(m p) o -> p m o", p=P)),
                        writes=[f"R5_{m}" for m in range(m0, m1)])

            tps = [(i, min(512, TH - i)) for i in range(0, TH, 512)]
            sgbuf = kv_sb[:, :].bitcast(F32)
            tb.alias_key("kvsb0", "kvsb")
            tb.alias_key("kvsb1", "kvsb")
            wgu_load(0)
            wdn_load(0)
            for ee in range(NE):
                for half in range(2):
                    base = half * TH
                    for m in range(ICH):
                        gb, gk = (ps0, "pb0") if m % 2 == 0 else (ps2, "pb2")
                        ub, uk_ = (ps1, "pb1") if m % 2 == 0 else (ps3, "pb3")
                        hrow = r6v(m * TH * 2, BF16, (TH,))
                        hk = r6k(m * TH * 2, TH * 2)
                        for (t0, tn) in tps:
                            for bank, bky, oc in ((gb, gk, 2 * m), (ub, uk_, 2 * m + 1)):
                                for j in range(CCH):
                                    tb.emit(T, lambda tt, j=j, oc=oc, t0=t0, tn=tn,
                                            bank=bank, base=base: tt.matmul(
                                                bank[:, 0:tn],
                                                r4[:, j, oc * P:(oc + 1) * P],
                                                r3[:, j, base + t0:base + t0 + tn],
                                                start=(j == 0), stop=(j == CCH - 1)),
                                            reads=[f"R4_{j}_{guk(oc)}", f"R3_{j}"],
                                            writes=([bky] if j in (0, CCH - 1) else []))
                            sg = sgbuf[:, (m % 2) * 512:(m % 2) * 512 + 512]
                            sgk = [f"kvsb{m % 2}"]
                            tb.emit(A, lambda a, sg=sg, gb=gb, tn=tn: a.activation(
                                sg[:, 0:tn], gb[:, 0:tn], AF.Silu),
                                reads=[gk], writes=sgk)
                            tb.emit(V, lambda v, sg=sg, ub=ub, hrow=hrow, t0=t0,
                                    tn=tn: v.tensor_tensor(
                                        hrow[:, t0:t0 + tn], sg[:, 0:tn],
                                        ub[:, 0:tn], op=ALU.mult),
                                    reads=sgk + [uk_], writes=hk)
                    for ti in range(nhalf):
                        t = half * nhalf + ti
                        for piece in range(2):
                            db = ps45[:, 0:512] if piece == 0 else ps45[:, 512:1024]
                            dbk = "pb4" if piece == 0 else "pb5"
                            for m in range(ICH):
                                hrow = r6v(m * TH * 2, BF16, (TH,))
                                hk = r6k(m * TH * 2, TH * 2)
                                tb.emit(T, lambda tt, m=m, ti=ti, piece=piece,
                                        db=db, hrow=hrow: tt.matmul(
                                            db, hrow[:, ti * P:(ti + 1) * P],
                                            r5[:, m, piece * 512:(piece + 1) * 512],
                                            start=(m == 0), stop=(m == ICH - 1)),
                                        reads=hk + [f"R5_{m}"],
                                        writes=([dbk] if m in (0, ICH - 1) else []))
                            tb.emit(V, lambda v, t=t, piece=piece, db=db, ee=ee:
                                    v.scalar_tensor_tensor(
                                        xv(t)[:, piece * 512:(piece + 1) * 512],
                                        db, w10[:, t, ee:ee + 1],
                                        xv(t)[:, piece * 512:(piece + 1) * 512],
                                        ALU.mult, ALU.add),
                                    reads=[dbk, "w10", xk(t)], writes=[xk(t)])
                if ee + 1 < NE:
                    wgu_load(ee + 1)
                    wdn_load(ee + 1)

            for t in range(tch):
                tb.dma(S, "dy", lambda s, t=t: s.dma_start(
                    out=y[t * P:(t + 1) * P, :], in_=xv(t)),
                    reads=[xk(t)])
            tb.eng["sp"].wait_ge(tb.sem["dy"], tb.dcnt["dy"])

    return nc


def _prep_inputs(L0_attn_x, L0_attn_w_qkv, L0_attn_w_proj, L0_attn_b_proj,
                 L0_attn_w_dwc, L0_attn_b_dwc, L0_ffn_gate_w,
                 L0_ffn_experts_gate_up, L0_ffn_experts_down,
                 L0_ffn_shared_gate_w, L0_ffn_shared_up_w, L0_ffn_shared_down_w,
                 L0_ffn_shared_expert_gate_w):
    f32 = np.float32
    x = np.ascontiguousarray(np.asarray(L0_attn_x, dtype=f32).reshape(-1, C))
    ntok = x.shape[0]
    per = ntok // NCORES
    wqkvT_ = np.ascontiguousarray(np.asarray(L0_attn_w_qkv, f32).T)
    wprojT_ = np.ascontiguousarray(np.asarray(L0_attn_w_proj, f32).T)
    gate9 = np.concatenate([np.asarray(L0_ffn_gate_w, f32),
                            np.asarray(L0_ffn_shared_expert_gate_w, f32)], axis=0)
    gate9T_ = np.ascontiguousarray(gate9.T)
    bprojr_ = np.ascontiguousarray(np.asarray(L0_attn_b_proj, f32).reshape(1, C))
    order = []
    for m in range(ICH):
        order += [m, ICH + m]
    egu = np.asarray(L0_ffn_experts_gate_up, f32)
    sg = np.asarray(L0_ffn_shared_gate_w, f32)
    su = np.asarray(L0_ffn_shared_up_w, f32)
    sd = np.asarray(L0_ffn_shared_down_w, f32)
    edn = np.asarray(L0_ffn_experts_down, f32)
    wguT_ = np.empty((NE, C, 2816), f32)
    wdnT_ = np.empty((NE, 1408, C), f32)
    for ee in range(8):
        rows = np.concatenate([egu[ee][o * P:(o + 1) * P] for o in order], axis=0)
        wguT_[ee] = rows.T
        wdnT_[ee] = edn[ee].T
    for pi in range(2):
        r0, r1 = pi * 1408, (pi + 1) * 1408
        stacked = np.concatenate([sg[r0:r1], su[r0:r1]], axis=0)
        rows = np.concatenate([stacked[o * P:(o + 1) * P] for o in order], axis=0)
        wguT_[8 + pi] = rows.T
        wdnT_[8 + pi] = np.ascontiguousarray(sd[:, r0:r1].T)
    wdc = np.asarray(L0_attn_w_dwc, f32)
    dwcw_ = np.ascontiguousarray(wdc.reshape(CCH, P, 3).transpose(1, 0, 2))
    dwcb_ = np.ascontiguousarray(np.asarray(L0_attn_b_dwc, f32).reshape(CCH, P).T)
    identw_ = np.eye(P, dtype=ml_dtypes.bfloat16)

    in_maps = []
    for i in range(NCORES):
        s = i * per
        hl = x[s - 1] if i > 0 else np.zeros(C, f32)
        hr = x[s + per] if i < NCORES - 1 else np.zeros(C, f32)
        in_maps.append({
            "xin": np.ascontiguousarray(x[s:s + per]),
            "xhalo": np.ascontiguousarray(np.stack([hl, hr])),
            "hmask": np.array([[1.0 if i > 0 else 0.0],
                               [1.0 if i < NCORES - 1 else 0.0]], f32),
            "wqkvT": wqkvT_, "wprojT": wprojT_, "gate9T": gate9T_,
            "bprojr": bprojr_, "wguT": wguT_, "wdnT": wdnT_,
            "dwcw": dwcw_, "dwcb": dwcb_, "identw": identw_,
        })
    return in_maps, x.shape


def kernel(**inputs):
    in_maps, (ntok, _) = _prep_inputs(**inputs)
    tch = ntok // NCORES // P
    nc = build_nc(tch=tch)
    res = run_bass_kernel_spmd(nc, in_maps, core_ids=list(range(NCORES)))
    out = np.concatenate([r["y"] for r in res.results], axis=0)
    return out.reshape(np.asarray(inputs["L0_attn_x"]).shape)
